# version 04
# baseline (speedup 1.0000x reference)
import contextlib

import numpy as np
import ml_dtypes

import concourse.bass as bass
import concourse.mybir as mybir
from concourse.bass_utils import run_bass_kernel_spmd

F32 = mybir.dt.float32
BF16 = mybir.dt.bfloat16
I32 = mybir.dt.int32
AF = mybir.ActivationFunctionType
ALU = mybir.AluOpType

NCORES = 8
D = 1024
NC = 8
H = 16
DH = 64
EPS = 1e-5

ENG = ("sync", "pe", "act", "dve", "pool")


class Sched:

    NSLOT = 4

    def __init__(self):
        self.q = {e: [] for e in ENG}
        self.cnt = {e: 0 for e in ENG}
        self.streams = {}

    def add(self, eng, fn, after=(), stream=None, inc=None):
        aft = tuple(t for t in after if t is not None)
        if stream is None:
            self.q[eng].append(("c", fn, aft, None, 0))
            self.cnt[eng] += 1
            return (eng, self.cnt[eng])
        stt_ = self.streams.setdefault(stream, {"n": 0, "cum": [0] * self.NSLOT})
        slot = stt_["n"] % self.NSLOT
        stt_["n"] += 1
        prev = stt_["cum"][slot]
        stt_["cum"][slot] = prev + 16
        name = "~%s%d" % (stream, slot)
        self.q[eng].append(("d", fn, aft, name, prev))
        return (name, prev + 16)

    def wait_only(self, eng, after):
        aft = tuple(t for t in after if t is not None)
        self.q[eng].append(("w", None, aft, None, 0))

    def barrier(self):
        out = [(e, self.cnt[e]) for e in ENG if self.cnt[e] > 0]
        for s, stt_ in self.streams.items():
            for slot in range(self.NSLOT):
                if stt_["cum"][slot] > 0:
                    out.append(("~%s%d" % (s, slot), stt_["cum"][slot]))
        return out

    def run(self, blk, nc, st):
        sems = {e: st.enter_context(nc.semaphore("sem_" + e)) for e in ENG}
        for s, stt_ in self.streams.items():
            for slot in range(self.NSLOT):
                if stt_["cum"][slot] > 0:
                    nm = "~%s%d" % (s, slot)
                    sems[nm] = st.enter_context(nc.semaphore("sem_s_%s%d" % (s, slot)))

        def runner(ename):
            def go(E):
                waited = {}

                def w(n, v):
                    if waited.get(n, 0) >= v:
                        return
                    E.wait_ge(sems[n], v)
                    waited[n] = v

                for kind, fn, aft, strm, cum in self.q[ename]:
                    for (n, v) in aft:
                        w(n, v)
                    if kind == "w":
                        continue
                    if kind == "d":
                        if cum > 0:
                            w(strm, cum)
                        fn(E).then_inc(sems[strm], 16)
                    else:
                        fn(E).then_inc(sems[ename], 1)
            return go

        if self.q["sync"]:
            blk.sync(runner("sync"))
        if self.q["pe"]:
            blk.tensor(runner("pe"))
        if self.q["act"]:
            blk.scalar(runner("act"))
        if self.q["dve"]:
            blk.vector(runner("dve"))
        if self.q["pool"]:
            blk.gpsimd(runner("pool"))


class LN:

    def __init__(self, S, sb, ps, T):
        self.S = S
        self.T = T
        self.ones_col = sb("ln_ones_col", [128, 1], BF16)
        self.ones_row = sb("ln_ones_row", [1, 128], F32)
        self.xbf = sb("ln_xbf", [128, NC * T], BF16)
        self.sq = sb("ln_sq", [128, NC * T], BF16)
        self.r0 = sb("ln_r0", [1, T], F32)
        self.r1 = sb("ln_r1", [1, T], F32)
        self.r2 = sb("ln_r2", [1, T], F32)
        self.ps_s0 = ps("ln_ps_s0", [1, T], F32)
        self.ps_s1 = ps("ln_ps_s1", [1, T], F32)
        self.ps_A = ps("ln_ps_A", [128, T], F32)
        self.ps_B = ps("ln_ps_B", [128, T], F32)
        t_i1 = S.add("dve", lambda E: E.memset(self.ones_col[:, :], 1.0))
        t_i2 = S.add("dve", lambda E: E.memset(self.ones_row[:, :], 1.0))
        self.free_toks = []
        self.prev_all = [t_i1, t_i2]

    def stats(self, x_sb, x_toks):
        S, T = self.S, self.T
        prev_free = list(self.free_toks)
        pa = list(self.prev_all)
        cast_t, sq_t = [], []
        for c in range(NC):
            xc = x_sb[:, c * T:(c + 1) * T]
            bc = self.xbf[:, c * T:(c + 1) * T]
            sc = self.sq[:, c * T:(c + 1) * T]
            aft = ([x_toks[c]] if x_toks else []) + pa
            cast_t.append(S.add("act", lambda E, bc=bc, xc=xc: E.activation(bc, xc, AF.Copy), after=aft))
            sq_t.append(S.add("act", lambda E, sc=sc, xc=xc: E.activation(sc, xc, AF.Square), after=aft))
        m1 = m2 = None
        for c in range(NC):
            bc = self.xbf[:, c * T:(c + 1) * T]
            sc = self.sq[:, c * T:(c + 1) * T]
            m1 = S.add("pe", lambda E, bc=bc, s=(c == 0), e=(c == NC - 1): E.matmul(
                self.ps_s0[0:1, :], self.ones_col[:, :], bc, start=s, stop=e),
                after=[cast_t[c]] + pa)
            m2 = S.add("pe", lambda E, sc=sc, s=(c == 0), e=(c == NC - 1): E.matmul(
                self.ps_s1[0:1, :], self.ones_col[:, :], sc, start=s, stop=e),
                after=[sq_t[c]])
        t_mean = S.add("dve", lambda E: E.tensor_scalar_mul(self.r0[:, :], self.ps_s0[0:1, :],
                                                            1.0 / D), after=[m1] + pa)
        t_msq = S.add("act", lambda E: E.activation(self.r2[:, :], self.r0[:, :], AF.Square),
                      after=[t_mean] + pa)
        t_var = S.add("dve", lambda E: E.scalar_tensor_tensor(
            self.r1[:, :], self.ps_s1[0:1, :], 1.0 / D, self.r2[:, :], ALU.mult, ALU.subtract),
            after=[m2, t_msq] + pa)
        t_vep = S.add("dve", lambda E: E.tensor_scalar_add(self.r2[:, :], self.r1[:, :], EPS),
                      after=[t_var, t_msq])
        t_lg = S.add("act", lambda E: E.activation(self.r1[:, :], self.r2[:, :], AF.Ln),
                     after=[t_vep])
        t_hf = S.add("dve", lambda E: E.tensor_scalar_mul(self.r2[:, :], self.r1[:, :], -0.5),
                     after=[t_lg])
        t_rstd = S.add("act", lambda E: E.activation(self.r1[:, :], self.r2[:, :], AF.Exp),
                       after=[t_hf])
        t_negm = S.add("dve", lambda E: E.scalar_tensor_tensor(
            self.r2[:, :], self.ps_s0[0:1, :], -1.0 / D, self.r1[:, :], ALU.mult, ALU.mult),
            after=[t_rstd] + pa)
        tokA = S.add("pe", lambda E: E.matmul(self.ps_A[:, :], self.ones_row[0:1, :], self.r1[0:1, :],
                                              start=True, stop=True), after=[t_rstd] + prev_free + pa)
        tokB = S.add("pe", lambda E: E.matmul(self.ps_B[:, :], self.ones_row[0:1, :], self.r2[0:1, :],
                                              start=True, stop=True), after=[t_negm] + pa)
        self.free_toks = []
        self.prev_all = [t_mean, t_msq, t_var, t_vep, t_lg, t_hf, t_rstd, t_negm, tokA, tokB]
        return tokA, tokB

    def note_use(self, tok):
        self.free_toks.append(tok)


def _std_setup(nc):
    S = Sched()
    st = contextlib.ExitStack()
    sb = lambda name, shape, dt: st.enter_context(nc.sbuf_tensor(name, shape, dt))
    ps = lambda name, shape, dt=F32: st.enter_context(nc.psum_tensor(name, shape, dt))
    return S, st, sb, ps


def _finish(nc, S, st, out_dma_toks):
    S.wait_only("pool", out_dma_toks if out_dma_toks else S.barrier())
    blk = st.enter_context(nc.Block())
    S.run(blk, nc, st)
    st.close()
    return nc



def build_A(TOK, T=512):
    nt = TOK // T
    nc = bass.Bass("TRN2", target_bir_lowering=False)
    xT = nc.dram_tensor("xT", [D, TOK], F32, kind="ExternalInput").ap()
    wkqvT = nc.dram_tensor("wkqvT", [D, 3 * D], F32, kind="ExternalInput").ap()
    wk_bf = nc.dram_tensor("wk_bf", [D, 3 * D], BF16, kind="Internal").ap()
    kqvT = nc.dram_tensor("kqvT", [3 * D, TOK], BF16, kind="ExternalOutput").ap()

    S, st, sb, ps = _std_setup(nc)
    ln = LN(S, sb, ps, T)
    x_sb = sb("x_sb", [128, NC * T], F32)
    xn = sb("xn", [128, NC * T], BF16)
    cbuf = [sb("cbuf0", [128, 3 * D], F32), sb("cbuf1", [128, 3 * D], F32)]
    cbf = [sb("cbf0", [128, 3 * D], BF16), sb("cbf1", [128, 3 * D], BF16)]
    wf = [sb("wf0", [128, NC * 128], BF16), sb("wf1", [128, NC * 128], BF16)]
    ob = [sb("ob0", [128, T], BF16), sb("ob1", [128, T], BF16)]
    ps_m = [ps("ps_m0", [128, T], F32), ps("ps_m1", [128, T], F32)]

    out_toks = []
    conv_done = []
    prev = {0: None, 1: None}
    prev_store = {0: None, 1: None}
    for i in range(NC):
        b = i % 2
        t_ld = S.add("sync", lambda E, i=i, b=b: E.dma_start(
            cbuf[b][:, :], wkqvT[i * 128:(i + 1) * 128, :]), after=[prev[b]], stream="lc")
        t_cv = S.add("act", lambda E, b=b: E.activation(cbf[b][:, :], cbuf[b][:, :], AF.Copy),
                     after=[t_ld, prev_store[b]])
        t_stv = S.add("pool", lambda E, i=i, b=b: E.dma_start(
            wk_bf[i * 128:(i + 1) * 128, :], cbf[b][:, :]), after=[t_cv], stream="sc")
        prev[b] = t_cv
        prev_store[b] = t_stv
        conv_done.append(t_stv)

    evac_t = {0: None, 1: None}
    ob_store = {0: None, 1: None}
    wf_used = {0: None, 1: None}
    prev_tile = []
    for t in range(nt):
        x_toks = []
        for c in range(NC):
            x_toks.append(S.add("sync", lambda E, c=c, t=t: E.dma_start(
                x_sb[:, c * T:(c + 1) * T], xT[c * 128:(c + 1) * 128, t * T:(t + 1) * T]),
                after=prev_tile if c == 0 else [], stream="lx"))
        tokA, tokB = ln.stats(x_sb, x_toks)
        xn_toks = []
        for c in range(NC):
            t1 = S.add("dve", lambda E, c=c: E.tensor_tensor(
                x_sb[:, c * T:(c + 1) * T], x_sb[:, c * T:(c + 1) * T], ln.ps_A[:, :], ALU.mult),
                after=[tokA, x_toks[c]])
            t2 = S.add("dve", lambda E, c=c: E.tensor_tensor(
                xn[:, c * T:(c + 1) * T], x_sb[:, c * T:(c + 1) * T], ln.ps_B[:, :], ALU.add),
                after=[tokB, t1])
            xn_toks.append(t2)
            ln.note_use(t2)
        for f in range(24):
            b = f % 2
            t_w = S.add("sync", lambda E, f=f, b=b: E.dma_start(
                wf[b][:, :].rearrange("p (c n) -> p c n", c=NC),
                wk_bf[:, f * 128:(f + 1) * 128].rearrange("(c p) n -> p c n", p=128)),
                after=[wf_used[b]] + (conv_done if (t == 0 and f < 2) else []), stream="lw")
            mm = None
            for e in range(NC):
                mm = S.add("pe", lambda E, e=e, b=b, s=(e == 0), st_=(e == NC - 1): E.matmul(
                    ps_m[b][:, :], wf[b][:, e * 128:(e + 1) * 128], xn[:, e * T:(e + 1) * T],
                    start=s, stop=st_),
                    after=([t_w, evac_t[b]] + xn_toks) if e == 0 else [])
            wf_used[b] = mm
            t_ev = S.add("act", lambda E, b=b: E.activation(ob[b][:, :], ps_m[b][:, :], AF.Copy),
                         after=[mm, ob_store[b]])
            evac_t[b] = t_ev
            t_st = S.add("pool", lambda E, f=f, t=t, b=b: E.dma_start(
                kqvT[f * 128:(f + 1) * 128, t * T:(t + 1) * T], ob[b][:, :]), after=[t_ev], stream="so")
            ob_store[b] = t_st
            out_toks.append(t_st)
        prev_tile = S.barrier()

    return _finish(nc, S, st, out_toks)



def build_B(npair=8, SEQ=4096, T=512):
    nm = SEQ // T
    nchunk = SEQ // 128
    nc = bass.Bass("TRN2", target_bir_lowering=False)
    kh = nc.dram_tensor("kh", [npair, DH, SEQ], BF16, kind="ExternalInput").ap()
    qh = nc.dram_tensor("qh", [npair, DH, SEQ], BF16, kind="ExternalInput").ap()
    vh = nc.dram_tensor("vh", [npair, DH, SEQ], BF16, kind="ExternalInput").ap()
    slv = nc.dram_tensor("slv", [128, npair], F32, kind="ExternalInput").ap()
    oh = nc.dram_tensor("oh", [npair, DH, SEQ], F32, kind="ExternalOutput").ap()

    S, st, sb, ps = _std_setup(nc)
    kT = [sb("kT0", [DH, SEQ], BF16), sb("kT1", [DH, SEQ], BF16)]
    qT = [sb("qT0", [DH, SEQ], BF16), sb("qT1", [DH, SEQ], BF16)]
    vT = [sb("vT0", [DH, SEQ], BF16), sb("vT1", [DH, SEQ], BF16)]
    v_ext = sb("v_ext", [128, 66 * nchunk], BF16)
    idm = sb("idm", [128, 128], BF16)
    ones_row = sb("b_ones_row", [1, 128], F32)
    iota_t = sb("iota_t", [128, T], F32)
    delta_t = sb("delta_t", [128, 32], F32)
    srel = sb("srel", [128, T], F32)
    bias_all = sb("bias_all", [128, 32], F32)
    slopes = sb("slopes", [128, npair], F32)
    u = [sb("u0", [128, T], F32), sb("u1", [128, T], F32)]
    p_bf = [sb("p0", [128, T], BF16), sb("p1", [128, T], BF16)]
    num_sb = sb("num_sb", [DH, T], F32)
    o_sb = sb("o_sb", [DH, T], F32)
    r_den = sb("r_den", [1, T], F32)
    r_rec = sb("r_rec", [1, T], F32)
    ps_s = [ps("ps_s0", [128, T], F32), ps("ps_s1", [128, T], F32)]
    ps_o = ps("ps_o", [65, T], F32)
    ps_tr = [ps("ps_tr0", [128, DH], BF16), ps("ps_tr1", [128, DH], BF16)]
    ps_bc = ps("ps_bc", [DH, T], F32)

    out_toks = []
    fillreg = {}

    def _fill(E):
        if "r" not in fillreg:
            fillreg["r"] = E.alloc_register("fillr")
            E.reg_mov(fillreg["r"], -1e30)
        return fillreg["r"]

    t_ones = S.add("dve", lambda E: E.memset(ones_row[:, :], 1.0))
    t_id0 = S.add("dve", lambda E: E.memset(idm[:, :], 1.0))
    t_id = S.add("pool", lambda E: E.affine_select(
        idm[:, :], idm[:, :], [[-1, 128]], ALU.is_equal, 0.0, base=0, channel_multiplier=1),
        after=[t_id0])
    t_iota = S.add("pool", lambda E: E.iota(
        iota_t[:, :], [[-1, T]], base=0, channel_multiplier=1,
        allow_small_or_imprecise_dtypes=True))
    t_dlt = S.add("pool", lambda E: E.iota(
        delta_t[:, :], [[128, 32]], base=-28 * 128, channel_multiplier=0,
        allow_small_or_imprecise_dtypes=True))
    t_slv = S.add("sync", lambda E: E.dma_start(slopes[:, :], slv[:, :]), stream="lm")
    t_vz = S.add("dve", lambda E: E.memset(v_ext[:, :], 0.0))
    vone_toks = []
    for ci in range(nchunk):
        vone_toks.append(S.add("dve", lambda E, ci=ci: E.memset(
            v_ext[:, ci * 66 + 64: ci * 66 + 65], 1.0), after=[t_vz]))

    ld_gate = {0: [], 1: []}
    for j in range(npair):
        b = j % 2
        pb = S.barrier() if j > 0 else []
        t_lk = S.add("sync", lambda E, j=j, b=b: E.dma_start(kT[b][:, :], kh[j, :, :]),
                     after=ld_gate[b], stream="lk")
        t_lq = S.add("sync", lambda E, j=j, b=b: E.dma_start(qT[b][:, :], qh[j, :, :]), stream="lq")
        t_lv = S.add("sync", lambda E, j=j, b=b: E.dma_start(vT[b][:, :], vh[j, :, :]), stream="lv")
        t_srel = S.add("dve", lambda E, j=j: E.tensor_scalar_mul(
            srel[:, :], iota_t[:, :], slopes[:, j:j + 1]), after=[t_iota, t_slv] + pb)
        t_bias = S.add("dve", lambda E, j=j: E.tensor_scalar_mul(
            bias_all[:, :], delta_t[:, :], slopes[:, j:j + 1]), after=[t_dlt, t_slv] + pb)
        prev_copy = {0: None, 1: None}
        tr_copy_toks = []
        for ci in range(nchunk):
            tb = ci % 2
            t_tr = S.add("pe", lambda E, ci=ci, b=b, tb=tb: E.transpose(
                ps_tr[tb][:, :], vT[b][:, ci * 128:(ci + 1) * 128], idm[0:DH, 0:DH]),
                after=[t_lv, t_id, prev_copy[tb]] + (pb if ci < 2 else []))
            t_cp = S.add("act", lambda E, ci=ci, tb=tb: E.activation(
                v_ext[:, ci * 66: ci * 66 + 64], ps_tr[tb][:, :], AF.Copy),
                after=[t_tr] + (vone_toks if ci == 0 else []) + (pb if ci < 2 else []))
            prev_copy[tb] = t_cp
            tr_copy_toks.append(t_cp)

        stt_done = {0: None, 1: None}
        exp_done = {0: None, 1: None}
        av_done = {0: None, 1: None}
        post_prev = []
        for m in range(nm):
            nch = 4 * m + 4
            t_av = None
            for c in range(nch):
                si = c % 2
                t_mm = S.add("pe", lambda E, c=c, m=m, b=b, si=si: E.matmul(
                    ps_s[si][:, :], kT[b][:, c * 128:(c + 1) * 128], qT[b][:, m * T:(m + 1) * T],
                    start=True, stop=True),
                    after=[t_lk, t_lq, stt_done[si]])
                t_stt = S.add("dve", lambda E, si=si: E.scalar_tensor_tensor(
                    u[si][:, :], ps_s[si][:, :], 1.0 / 32.0, srel[:, :], ALU.mult, ALU.add),
                    after=[t_mm, t_srel, exp_done[si]])
                stt_done[si] = t_stt
                delta = 128 * c - T * m
                idx = (delta // 128) + 28
                t_msk = None
                if c >= 4 * m:
                    t_msk = S.add("pool", lambda E, si=si, delta=delta: E.affine_select(
                        u[si][:, :], u[si][:, :], [[1, T]], ALU.is_ge, _fill(E),
                        base=-delta, channel_multiplier=-1), after=[t_stt])
                t_exp = S.add("act", lambda E, si=si, idx=idx: E.activation(
                    p_bf[si][:, :], u[si][:, :], AF.Exp, bias=bias_all[:, idx:idx + 1]),
                    after=[t_stt, t_msk, t_bias, av_done[si]])
                exp_done[si] = t_exp
                t_av = S.add("pe", lambda E, c=c, si=si, s=(c == 0), e=(c == nch - 1): E.matmul(
                    ps_o[:, :], v_ext[:, c * 66: c * 66 + 65], p_bf[si][:, :], start=s, stop=e),
                    after=[t_exp] + (tr_copy_toks + post_prev if c == 0 else []))
                av_done[si] = t_av
            t_den = S.add("act", lambda E: E.activation(r_den[:, :], ps_o[64:65, :], AF.Copy),
                          after=[t_av] + post_prev)
            t_rec = S.add("dve", lambda E: E.reciprocal(r_rec[:, :], r_den[:, :]),
                          after=[t_den] + post_prev)
            t_bc = S.add("pe", lambda E: E.matmul(ps_bc[:, :], ones_row[0:1, 0:DH], r_rec[0:1, :],
                                                  start=True, stop=True),
                         after=[t_rec, t_ones] + post_prev)
            t_num = S.add("act", lambda E: E.activation(num_sb[:, :], ps_o[0:DH, :], AF.Copy),
                          after=[t_av] + post_prev)
            t_mul = S.add("dve", lambda E: E.tensor_tensor(o_sb[:, :], num_sb[:, :], ps_bc[0:DH, :],
                                                           ALU.mult), after=[t_bc, t_num] + post_prev)
            t_st = S.add("pool", lambda E, j=j, m=m: E.dma_start(
                oh[j, :, m * T:(m + 1) * T], o_sb[:, :]), after=[t_mul], stream="so")
            out_toks.append(t_st)
            post_prev = [t_den, t_rec, t_bc, t_num, t_mul, t_st]
        ld_gate[b] = S.barrier()

    return _finish(nc, S, st, out_toks)



def build_C(TOK, T=512):
    nt = TOK // T
    FF0 = 3072
    nf = FF0 // 128
    nc = bass.Bass("TRN2", target_bir_lowering=False)
    xT = nc.dram_tensor("xT", [D, TOK], F32, kind="ExternalInput").ap()
    atT = nc.dram_tensor("atT", [D, TOK], F32, kind="ExternalInput").ap()
    wgT = nc.dram_tensor("wgT", [D, FF0], I32, kind="ExternalInput").ap()
    wuT = nc.dram_tensor("wuT", [D, FF0], I32, kind="ExternalInput").ap()
    wdT = nc.dram_tensor("wdT", [FF0, D], I32, kind="ExternalInput").ap()
    wq = nc.dram_tensor("wq", [D, D], F32, kind="ExternalInput").ap()
    wk = nc.dram_tensor("wk", [D, D], F32, kind="ExternalInput").ap()
    wv = nc.dram_tensor("wv", [D, D], F32, kind="ExternalInput").ap()
    proj = nc.dram_tensor("proj", [DH, DH], F32, kind="ExternalInput").ap()
    g1v = nc.dram_tensor("g1v", [128, NC], F32, kind="ExternalInput").ap()
    b1v = nc.dram_tensor("b1v", [128, NC], F32, kind="ExternalInput").ap()
    wg_bf = nc.dram_tensor("wg_bf", [D, FF0], BF16, kind="Internal").ap()
    wu_bf = nc.dram_tensor("wu_bf", [D, FF0], BF16, kind="Internal").ap()
    wd_bf = nc.dram_tensor("wd_bf", [FF0, D], BF16, kind="Internal").ap()
    wq_bf = nc.dram_tensor("wq_bf", [D, D], BF16, kind="Internal").ap()
    wk_bf = nc.dram_tensor("wk_bf", [D, D], BF16, kind="Internal").ap()
    wv_bf = nc.dram_tensor("wv_bf", [D, D], BF16, kind="Internal").ap()
    x2T = nc.dram_tensor("x2T", [D, TOK], F32, kind="ExternalOutput").ap()
    xinT = nc.dram_tensor("xinT", [D, TOK], F32, kind="ExternalOutput").ap()
    qpT = nc.dram_tensor("qpT", [D, TOK], BF16, kind="ExternalOutput").ap()
    kpT = nc.dram_tensor("kpT", [D, TOK], BF16, kind="ExternalOutput").ap()
    vpT = nc.dram_tensor("vpT", [D, TOK], BF16, kind="ExternalOutput").ap()

    S, st, sb, ps = _std_setup(nc)
    ln = LN(S, sb, ps, T)
    x_sb = sb("x_sb", [128, NC * T], F32)
    at_sb = sb("at_sb", [128, NC * T], F32)
    x1 = sb("x1", [128, NC * T], F32)
    x2 = sb("x2", [128, NC * T], F32)
    xin = sb("xin", [128, NC * T], F32)
    xn1 = sb("xn1", [128, NC * T], BF16)
    h1 = sb("h1", [128, NC * T], BF16)
    qt = sb("qt", [128, NC * T], BF16)
    kt = sb("kt", [128, NC * T], BF16)
    hh = sb("hh", [128, nf * T], BF16)
    sg = sb("sg", [128, T], F32)
    tmp = sb("tmpc", [128, T], F32)
    CW = 1536
    ci32 = sb("ci32", [128, CW], I32)
    cf32 = sb("cf32", [128, CW], F32)
    cbf = sb("cbf", [128, CW], BF16)
    wfb = [sb("wfb0", [128, NC * 128], BF16), sb("wfb1", [128, NC * 128], BF16)]
    wfb2 = [sb("wfb2_0", [128, NC * 128], BF16), sb("wfb2_1", [128, NC * 128], BF16)]
    wdb = [sb("wdb0", [128, 128], BF16), sb("wdb1", [128, 128], BF16)]
    projb = sb("projb", [128, DH], BF16)
    projf = sb("projf", [128, DH], F32)
    g1s = sb("g1s", [128, NC], F32)
    b1s = sb("b1s", [128, NC], F32)
    ph_a = sb("ph_a", [DH, T], F32)
    ph_b = sb("ph_b", [DH, T], F32)
    ph_e = sb("ph_e", [DH, T], F32)
    ob = [sb("obc0", [128, T], BF16), sb("obc1", [128, T], BF16)]
    ps_m = [ps("cps_m0", [128, T], F32), ps("cps_m1", [128, T], F32)]
    ps_phi = ps("ps_phi", [DH, T], F32)

    out_toks = []
    cstate = {"cv": None, "st": None}

    def conv(dst, src, nblk, width, scale, toks, is_int):
        for i in range(nblk):
            for j0 in range(0, width, CW):
                w_ = min(CW, width - j0)
                t_ld = S.add("sync", lambda E, i=i, j0=j0, w_=w_, src=src, is_int=is_int: E.dma_start(
                    (ci32 if is_int else cf32)[:, 0:w_], src[i * 128:(i + 1) * 128, j0:j0 + w_]),
                    after=[cstate["cv"]], stream="lc")
                if is_int:
                    t_c1 = S.add("dve", lambda E, w_=w_: E.tensor_copy(
                        cf32[:, 0:w_], ci32[:, 0:w_]), after=[t_ld, cstate["st"]])
                    t_c2 = S.add("dve", lambda E, w_=w_, scale=scale: E.tensor_scalar_mul(
                        cbf[:, 0:w_], cf32[:, 0:w_], scale), after=[t_c1, cstate["st"]])
                    cstate["cv"] = t_c2
                else:
                    t_c2 = S.add("act", lambda E, w_=w_: E.activation(
                        cbf[:, 0:w_], cf32[:, 0:w_], AF.Copy), after=[t_ld, cstate["st"]])
                    cstate["cv"] = t_c2
                t_st2 = S.add("pool", lambda E, i=i, j0=j0, w_=w_, dst=dst: E.dma_start(
                    dst[i * 128:(i + 1) * 128, j0:j0 + w_], cbf[:, 0:w_]), after=[t_c2], stream="sc")
                cstate["st"] = t_st2
                toks.append(t_st2)

    conv_g, conv_u, conv_d, conv_q, conv_k, conv_v = [], [], [], [], [], []
    conv(wg_bf, wgT, NC, FF0, 1.0 / 127.0, conv_g, True)
    conv(wu_bf, wuT, NC, FF0, 1.0 / 127.0, conv_u, True)
    conv(wd_bf, wdT, nf, D, 1.0 / 127.0, conv_d, True)
    conv(wq_bf, wq, NC, D, 1.0, conv_q, False)
    conv(wk_bf, wk, NC, D, 1.0, conv_k, False)
    conv(wv_bf, wv, NC, D, 1.0, conv_v, False)
    t_pl0 = S.add("sync", lambda E: E.dma_start(projf[0:DH, :], proj[:, :]), stream="lm")
    t_pl1 = S.add("sync", lambda E: E.dma_start(projf[DH:128, :], proj[:, :]), stream="lm")
    t_pb = S.add("act", lambda E: E.activation(projb[:, :], projf[:, :], AF.Copy),
                 after=[t_pl0, t_pl1])
    t_g1 = S.add("sync", lambda E: E.dma_start(g1s[:, :], g1v[:, :]), stream="lm")
    t_b1 = S.add("sync", lambda E: E.dma_start(b1s[:, :], b1v[:, :]), stream="lm")

    def apply_ln(src_sb, dst_sb, tokA, tokB, src_toks, extra=None, extra_toks=None, gb=None):
        toks = []
        prev = [None]
        for c in range(NC):
            s_ap = src_sb[:, c * T:(c + 1) * T]
            d_ap = dst_sb[:, c * T:(c + 1) * T]
            t1 = S.add("dve", lambda E, s_ap=s_ap: E.tensor_tensor(tmp[:, :], s_ap, ln.ps_A[:, :],
                                                                   ALU.mult),
                       after=[tokA, prev[0]] + ([src_toks[c]] if src_toks else []))
            if gb is not None:
                t2 = S.add("dve", lambda E: E.tensor_tensor(tmp[:, :], tmp[:, :], ln.ps_B[:, :],
                                                            ALU.add), after=[tokB, t1])
                gs, bs = gb
                t3 = S.add("dve", lambda E, c=c, d_ap=d_ap, gs=gs, bs=bs: E.tensor_scalar(
                    d_ap, tmp[:, :], gs[:, c:c + 1], bs[:, c:c + 1], ALU.mult, ALU.add),
                    after=[t2, t_g1, t_b1])
            elif extra is not None:
                t2 = S.add("dve", lambda E: E.tensor_tensor(tmp[:, :], tmp[:, :], ln.ps_B[:, :],
                                                            ALU.add), after=[tokB, t1])
                e_ap = extra[:, c * T:(c + 1) * T]
                t3 = S.add("dve", lambda E, d_ap=d_ap, e_ap=e_ap: E.tensor_tensor(
                    d_ap, tmp[:, :], e_ap, ALU.add),
                    after=[t2] + ([extra_toks[c]] if extra_toks else []))
            else:
                t3 = S.add("dve", lambda E, d_ap=d_ap: E.tensor_tensor(
                    d_ap, tmp[:, :], ln.ps_B[:, :], ALU.add), after=[tokB, t1])
            ln.note_use(t3)
            prev[0] = t3
            toks.append(t3)
        return toks

    evac = {0: None, 1: None}
    ob_st = {0: None, 1: None}
    wf_used = {0: None, 1: None}
    wf2_used = {0: None, 1: None}
    wdb_used = {0: None, 1: None}
    hh_toks = [None] * nf
    prev_tile = []
    for t in range(nt):
        x_toks, a_toks = [], []
        for c in range(NC):
            x_toks.append(S.add("sync", lambda E, c=c, t=t: E.dma_start(
                x_sb[:, c * T:(c + 1) * T], xT[c * 128:(c + 1) * 128, t * T:(t + 1) * T]),
                after=prev_tile if c == 0 else [], stream="lx"))
            a_toks.append(S.add("sync", lambda E, c=c, t=t: E.dma_start(
                at_sb[:, c * T:(c + 1) * T], atT[c * 128:(c + 1) * 128, t * T:(t + 1) * T]), stream="la"))
        tokA, tokB = ln.stats(at_sb, a_toks)
        x1_toks = apply_ln(at_sb, x1, tokA, tokB, a_toks, extra=x_sb, extra_toks=x_toks)
        tokA, tokB = ln.stats(x1, x1_toks)
        xn1_toks = apply_ln(x1, xn1, tokA, tokB, x1_toks)
        for f in range(nf):
            b = f % 2
            t_wg = S.add("sync", lambda E, f=f, b=b: E.dma_start(
                wfb[b][:, :].rearrange("p (c n) -> p c n", c=NC),
                wg_bf[:, f * 128:(f + 1) * 128].rearrange("(c p) n -> p c n", p=128)),
                after=[wf_used[b]] + (conv_g if (t == 0 and f < 2) else []), stream="lwg")
            t_wu = S.add("sync", lambda E, f=f, b=b: E.dma_start(
                wfb2[b][:, :].rearrange("p (c n) -> p c n", c=NC),
                wu_bf[:, f * 128:(f + 1) * 128].rearrange("(c p) n -> p c n", p=128)),
                after=[wf2_used[b]] + (conv_u if (t == 0 and f < 2) else []), stream="lwu")
            mmg = mmu = None
            for e in range(NC):
                mmg = S.add("pe", lambda E, e=e, b=b, s=(e == 0), st_=(e == NC - 1): E.matmul(
                    ps_m[0][:, :], wfb[b][:, e * 128:(e + 1) * 128], xn1[:, e * T:(e + 1) * T],
                    start=s, stop=st_),
                    after=([t_wg, evac[0]] + xn1_toks) if e == 0 else [])
            for e in range(NC):
                mmu = S.add("pe", lambda E, e=e, b=b, s=(e == 0), st_=(e == NC - 1): E.matmul(
                    ps_m[1][:, :], wfb2[b][:, e * 128:(e + 1) * 128], xn1[:, e * T:(e + 1) * T],
                    start=s, stop=st_),
                    after=[t_wu, evac[1]] if e == 0 else [])
            wf_used[b] = mmg
            wf2_used[b] = mmu
            t_sg = S.add("act", lambda E: E.activation(sg[:, :], ps_m[0][:, :], AF.Silu),
                         after=[mmg, evac[0], evac[1]])
            evac[0] = t_sg
            t_hh = S.add("dve", lambda E, f=f: E.tensor_tensor(
                hh[:, f * T:(f + 1) * T], sg[:, :], ps_m[1][:, :], ALU.mult), after=[t_sg, mmu])
            evac[1] = t_hh
            hh_toks[f] = t_hh
        dn_toks = []
        for dd in range(NC):
            mmd = None
            for f in range(nf):
                b = f % 2
                t_wd = S.add("sync", lambda E, dd=dd, f=f, b=b: E.dma_start(
                    wdb[b][:, :], wd_bf[f * 128:(f + 1) * 128, dd * 128:(dd + 1) * 128]),
                    after=[wdb_used[b]] + (conv_d if (t == 0 and dd == 0 and f < 2) else []), stream="lwd")
                mmd = S.add("pe", lambda E, f=f, b=b, s=(f == 0), st_=(f == nf - 1): E.matmul(
                    ps_m[0][:, :], wdb[b][:, :], hh[:, f * T:(f + 1) * T], start=s, stop=st_),
                    after=[t_wd, hh_toks[f]] + ([evac[0]] if f == 0 else []))
                wdb_used[b] = mmd
            t_ev = S.add("act", lambda E, dd=dd: E.activation(
                at_sb[:, dd * T:(dd + 1) * T], ps_m[0][:, :], AF.Copy), after=[mmd])
            evac[0] = t_ev
            dn_toks.append(t_ev)
        tokA, tokB = ln.stats(at_sb, dn_toks)
        x2_toks = apply_ln(at_sb, x2, tokA, tokB, dn_toks, extra=x1, extra_toks=x1_toks)
        for c in range(NC):
            out_toks.append(S.add("pool", lambda E, c=c, t=t: E.dma_start(
                x2T[c * 128:(c + 1) * 128, t * T:(t + 1) * T], x2[:, c * T:(c + 1) * T]),
                after=[x2_toks[c]], stream="sx2"))
        tokA, tokB = ln.stats(x2, x2_toks)
        xin_toks = apply_ln(x2, xin, tokA, tokB, x2_toks)
        for c in range(NC):
            out_toks.append(S.add("pool", lambda E, c=c, t=t: E.dma_start(
                xinT[c * 128:(c + 1) * 128, t * T:(t + 1) * T], xin[:, c * T:(c + 1) * T]),
                after=[xin_toks[c]], stream="sxi"))
        tokA, tokB = ln.stats(xin, xin_toks)
        h1_toks = apply_ln(xin, h1, tokA, tokB, xin_toks, gb=(g1s, b1s))
        for wsrc, wconv, kind in ((wq_bf, conv_q, "q"), (wk_bf, conv_k, "k"), (wv_bf, conv_v, "v")):
            for dd in range(NC):
                b = dd % 2
                t_w = S.add("sync", lambda E, dd=dd, b=b, wsrc=wsrc: E.dma_start(
                    wfb[b][:, :].rearrange("p (c n) -> p c n", c=NC),
                    wsrc[:, dd * 128:(dd + 1) * 128].rearrange("(c p) n -> p c n", p=128)),
                    after=[wf_used[b]] + (wconv if (t == 0 and dd < 2) else []), stream="lw")
                mm = None
                for e in range(NC):
                    mm = S.add("pe", lambda E, e=e, b=b, s=(e == 0), st_=(e == NC - 1): E.matmul(
                        ps_m[0][:, :], wfb[b][:, e * 128:(e + 1) * 128], h1[:, e * T:(e + 1) * T],
                        start=s, stop=st_),
                        after=([t_w, evac[0]] + h1_toks) if e == 0 else [])
                wf_used[b] = mm
                bo = dd % 2
                if kind == "v":
                    t_ev = S.add("act", lambda E, bo=bo: E.activation(ob[bo][:, :], ps_m[0][:, :],
                                                                      AF.Copy),
                                 after=[mm, ob_st[bo]])
                    evac[0] = t_ev
                    t_s = S.add("pool", lambda E, dd=dd, t=t, bo=bo: E.dma_start(
                        vpT[dd * 128:(dd + 1) * 128, t * T:(t + 1) * T], ob[bo][:, :]),
                        after=[t_ev], stream="svp")
                    ob_st[bo] = t_s
                    out_toks.append(t_s)
                else:
                    dst = qt if kind == "q" else kt
                    t_ev = S.add("act", lambda E, dd=dd, dst=dst: E.activation(
                        dst[:, dd * T:(dd + 1) * T], ps_m[0][:, :], AF.Copy), after=[mm])
                    evac[0] = t_ev
        bar = S.barrier()
        phi_prev = []
        for src_sb, dst_dram in ((qt, qpT), (kt, kpT)):
            for hd in range(H):
                cchunk = hd // 2
                poff = (hd % 2) * DH
                rhs = src_sb[poff:poff + DH, cchunk * T:(cchunk + 1) * T]
                mm = S.add("pe", lambda E, rhs=rhs, poff=poff: E.matmul(
                    ps_phi[:, :], projb[poff:poff + DH, :], rhs, start=True, stop=True),
                    after=bar + [t_pb] + phi_prev)
                t_a = S.add("dve", lambda E: E.tensor_scalar_min(ph_a[:, :], ps_phi[:, :], 0.0),
                            after=[mm] + phi_prev)
                t_b2 = S.add("dve", lambda E: E.tensor_scalar_max(ph_b[:, :], ps_phi[:, :], 0.0),
                             after=[mm])
                t_e = S.add("act", lambda E: E.activation(ph_e[:, :], ph_a[:, :], AF.Exp),
                            after=[t_a] + phi_prev)
                bo = hd % 2
                t_o = S.add("dve", lambda E, bo=bo: E.tensor_tensor(
                    ob[bo][0:DH, :], ph_e[:, :], ph_b[:, :], ALU.add),
                    after=[t_e, t_b2, ob_st[bo]])
                t_s = S.add("pool", lambda E, hd=hd, t=t, bo=bo, dst_dram=dst_dram: E.dma_start(
                    dst_dram[hd * DH:(hd + 1) * DH, t * T:(t + 1) * T], ob[bo][0:DH, :]),
                    after=[t_o], stream="sqp")
                ob_st[bo] = t_s
                phi_prev = [t_a, t_b2, t_o, t_e]
                out_toks.append(t_s)
        prev_tile = S.barrier()

    return _finish(nc, S, st, out_toks)



def build_D(npair=8, SEQ=4096):
    CH = 128
    nchunk = SEQ // CH
    nc = bass.Bass("TRN2", target_bir_lowering=False)
    qh = nc.dram_tensor("qh", [npair, DH, SEQ], BF16, kind="ExternalInput").ap()
    kh = nc.dram_tensor("kh", [npair, DH, SEQ], BF16, kind="ExternalInput").ap()
    vh = nc.dram_tensor("vh", [npair, DH, SEQ], BF16, kind="ExternalInput").ap()
    oh = nc.dram_tensor("oh", [npair, DH, SEQ], BF16, kind="ExternalOutput").ap()

    S, st, sb, ps = _std_setup(nc)
    qT = [sb("qT0", [DH, SEQ], BF16), sb("qT1", [DH, SEQ], BF16)]
    kT = [sb("kT0", [DH, SEQ], BF16), sb("kT1", [DH, SEQ], BF16)]
    vT = [sb("vT0", [DH, SEQ], BF16), sb("vT1", [DH, SEQ], BF16)]
    o_all = sb("o_all", [DH, SEQ], BF16)
    idm = sb("idm", [128, 128], BF16)
    maskt = sb("maskt", [128, CH], F32)
    ones_row = sb("d_ones_row", [1, 128], F32)
    v_ext = sb("v_ext", [128, DH + 1], BF16)
    k_tok = sb("k_tok", [128, DH], BF16)
    p_lin = sb("p_lin", [128, CH], BF16)
    kv_f = sb("kv_f", [DH, DH + 1], F32)
    kv_b = sb("kv_b", [DH, DH + 1], BF16)
    num_sb = sb("num_sb", [DH, CH], F32)
    r_den = sb("r_den", [1, CH], F32)
    r_rec = sb("r_rec", [1, CH], F32)
    ps_trv = ps("ps_trv", [128, DH], BF16)
    ps_trk = ps("ps_trk", [128, DH], BF16)
    ps_sl = ps("ps_sl", [128, CH], F32)
    ps_o = ps("ps_o", [DH + 1, CH], F32)
    ps_kv = ps("ps_kv", [DH, DH + 1], F32)
    ps_bc = ps("ps_bc", [DH, CH], F32)

    out_toks = []
    t_ones = S.add("dve", lambda E: E.memset(ones_row[:, :], 1.0))
    t_id0 = S.add("dve", lambda E: E.memset(idm[:, :], 1.0))
    t_id = S.add("pool", lambda E: E.affine_select(
        idm[:, :], idm[:, :], [[-1, 128]], ALU.is_equal, 0.0, base=0, channel_multiplier=1),
        after=[t_id0])
    t_m0 = S.add("dve", lambda E: E.memset(maskt[:, :], 1.0))
    t_mask = S.add("pool", lambda E: E.affine_select(
        maskt[:, :], maskt[:, :], [[1, CH]], ALU.is_ge, 0.0, base=0, channel_multiplier=-1),
        after=[t_m0])
    t_vone = S.add("dve", lambda E: E.memset(v_ext[:, DH:DH + 1], 1.0))

    ld_gate = {0: [], 1: []}
    last_store = None
    for j in range(npair):
        b = j % 2
        pb = S.barrier() if j > 0 else []
        t_lq = S.add("sync", lambda E, j=j, b=b: E.dma_start(qT[b][:, :], qh[j, :, :]),
                     after=ld_gate[b], stream="lq")
        t_lk = S.add("sync", lambda E, j=j, b=b: E.dma_start(kT[b][:, :], kh[j, :, :]), stream="lk")
        t_lv = S.add("sync", lambda E, j=j, b=b: E.dma_start(vT[b][:, :], vh[j, :, :]), stream="lv")
        t_z1 = S.add("dve", lambda E: E.memset(kv_f[:, :], 0.0), after=pb)
        t_kvb = S.add("act", lambda E: E.activation(kv_b[:, :], kv_f[:, :], AF.Copy),
                      after=[t_z1] + pb)
        t_kvadd = t_z1

        cpv_gate = pb
        cpk_gate = pb
        sl_gate = pb
        o_gate = pb
        for i in range(nchunk):
            sl = slice(i * CH, (i + 1) * CH)
            t_trv = S.add("pe", lambda E, b=b, sl=sl: E.transpose(
                ps_trv[:, :], vT[b][:, sl], idm[0:DH, 0:DH]),
                after=[t_lv, t_id] + cpv_gate)
            t_cpv = S.add("act", lambda E: E.activation(v_ext[:, 0:DH], ps_trv[:, :], AF.Copy),
                          after=[t_trv, t_vone] + cpv_gate)
            t_trk = S.add("pe", lambda E, b=b, sl=sl: E.transpose(
                ps_trk[:, :], kT[b][:, sl], idm[0:DH, 0:DH]), after=[t_lk, t_id] + cpk_gate)
            t_cpk = S.add("act", lambda E: E.activation(k_tok[:, :], ps_trk[:, :], AF.Copy),
                          after=[t_trk] + cpk_gate)
            t_sl = S.add("pe", lambda E, b=b, sl=sl: E.matmul(
                ps_sl[:, :], kT[b][:, sl], qT[b][:, sl], start=True, stop=True),
                after=[t_lk, t_lq] + sl_gate)
            t_pl = S.add("dve", lambda E: E.tensor_tensor(p_lin[:, :], ps_sl[:, :], maskt[:, :],
                                                          ALU.mult),
                         after=[t_sl, t_mask] + sl_gate)
            sl_gate = []
            t_o1 = S.add("pe", lambda E: E.matmul(ps_o[:, :], v_ext[:, :], p_lin[:, :],
                                                  start=True, stop=False),
                         after=[t_pl, t_cpv] + o_gate)
            t_o2 = S.add("pe", lambda E, b=b, sl=sl: E.matmul(
                ps_o[:, :], kv_b[:, :], qT[b][:, sl], start=False, stop=True),
                after=[t_o1, t_kvb])
            t_kv = S.add("pe", lambda E: E.matmul(ps_kv[:, :], k_tok[:, :], v_ext[:, :],
                                                  start=True, stop=True),
                         after=[t_cpk, t_cpv, t_kvadd])
            t_kvadd = S.add("dve", lambda E: E.tensor_tensor(kv_f[:, :], kv_f[:, :], ps_kv[:, :],
                                                             ALU.add), after=[t_kv, t_kvb])
            t_kvb = S.add("act", lambda E: E.activation(kv_b[:, :], kv_f[:, :], AF.Copy),
                          after=[t_kvadd, t_o2])
            cpv_gate = [t_o1, t_kv]
            cpk_gate = [t_kv]
            t_den = S.add("dve", lambda E: E.tensor_scalar_add(
                r_den[:, :], ps_o[DH:DH + 1, :], 1e-6), after=[t_o2] + o_gate)
            t_rec = S.add("dve", lambda E: E.reciprocal(r_rec[:, :], r_den[:, :]),
                          after=[t_den] + o_gate)
            t_bc = S.add("pe", lambda E: E.matmul(ps_bc[:, :], ones_row[0:1, 0:DH], r_rec[0:1, :],
                                                  start=True, stop=True),
                         after=[t_rec, t_ones] + o_gate)
            t_num = S.add("act", lambda E: E.activation(num_sb[:, :], ps_o[0:DH, :], AF.Copy),
                          after=[t_o2] + o_gate)
            t_mul = S.add("dve", lambda E, sl=sl: E.tensor_tensor(
                o_all[:, sl], num_sb[:, :], ps_bc[:, :], ALU.mult),
                after=[t_bc, t_num, last_store])
            sl_gate = [t_pl]
            o_gate = [t_den, t_rec, t_bc, t_num, t_mul]
        t_st = S.add("pool", lambda E, j=j: E.dma_start(oh[j, :, :], o_all[:, :]),
                     after=o_gate, stream="so")
        last_store = t_st
        out_toks.append(t_st)
        ld_gate[b] = S.barrier()

    return _finish(nc, S, st, out_toks)



def build_E(TOK, T=512):
    nt = TOK // T
    FF1 = 4096
    nf = FF1 // 128
    nc = bass.Bass("TRN2", target_bir_lowering=False)
    x2T = nc.dram_tensor("x2T", [D, TOK], F32, kind="ExternalInput").ap()
    xinT = nc.dram_tensor("xinT", [D, TOK], F32, kind="ExternalInput").ap()
    alT = nc.dram_tensor("alT", [D, TOK], BF16, kind="ExternalInput").ap()
    wo = nc.dram_tensor("wo", [D, D], F32, kind="ExternalInput").ap()
    w1 = nc.dram_tensor("w1", [D, FF1], F32, kind="ExternalInput").ap()
    w2 = nc.dram_tensor("w2", [FF1, D], F32, kind="ExternalInput").ap()
    wsT = nc.dram_tensor("wsT", [D, D], F32, kind="ExternalInput").ap()
    b1v = nc.dram_tensor("b1v", [128, nf], F32, kind="ExternalInput").ap()
    b2v = nc.dram_tensor("b2v", [128, NC], F32, kind="ExternalInput").ap()
    g2v = nc.dram_tensor("g2v", [128, NC], F32, kind="ExternalInput").ap()
    be2v = nc.dram_tensor("be2v", [128, NC], F32, kind="ExternalInput").ap()
    wo_bf = nc.dram_tensor("wo_bf", [D, D], BF16, kind="Internal").ap()
    w1_bf = nc.dram_tensor("w1_bf", [D, FF1], BF16, kind="Internal").ap()
    w2_bf = nc.dram_tensor("w2_bf", [FF1, D], BF16, kind="Internal").ap()
    x4T = nc.dram_tensor("x4T", [D, TOK], F32, kind="ExternalOutput").ap()

    S, st, sb, ps = _std_setup(nc)
    ln = LN(S, sb, ps, T)
    xin = sb("xin", [128, NC * T], F32)
    al = sb("al", [128, NC * T], BF16)
    x2 = sb("x2", [128, NC * T], F32)
    xmid = sb("xmid", [128, NC * T], F32)
    xmid2 = sb("xmid2", [128, NC * T], F32)
    h2 = None
    x3 = sb("x3", [128, NC * T], F32)
    xin2 = sb("xin2", [128, NC * T], BF16)
    aall = sb("aall", [128, nf * T], BF16)
    sgn = sb("sgn", [128, NC * D], BF16)
    tmp = sb("tmpe", [128, T], F32)
    CW = 2048
    cf32 = sb("cf32", [128, CW], F32)
    cbf = sb("cbf", [128, CW], BF16)
    wfb = [sb("wfb0", [128, NC * 128], BF16), sb("wfb1", [128, NC * 128], BF16)]
    wdb = [sb("wdb0", [128, 128], BF16), sb("wdb1", [128, 128], BF16)]
    b1s = sb("b1s", [128, nf], F32)
    b2s = sb("b2s", [128, NC], F32)
    h2 = al
    g2s = sb("g2s", [128, NC], F32)
    be2s = sb("be2s", [128, NC], F32)
    ps_m = [ps("eps_m0", [128, T], F32), ps("eps_m1", [128, T], F32)]

    out_toks = []
    cstate = {"cv": None, "st": None}

    def convf(dst, src, nblk, width, toks):
        for i in range(nblk):
            for j0 in range(0, width, CW):
                w_ = min(CW, width - j0)
                t_ld = S.add("sync", lambda E, i=i, j0=j0, w_=w_, src=src: E.dma_start(
                    cf32[:, 0:w_], src[i * 128:(i + 1) * 128, j0:j0 + w_]),
                    after=[cstate["cv"]], stream="lc")
                t_c2 = S.add("act", lambda E, w_=w_: E.activation(
                    cbf[:, 0:w_], cf32[:, 0:w_], AF.Copy), after=[t_ld, cstate["st"]])
                cstate["cv"] = t_c2
                t_st2 = S.add("pool", lambda E, i=i, j0=j0, w_=w_, dst=dst: E.dma_start(
                    dst[i * 128:(i + 1) * 128, j0:j0 + w_], cbf[:, 0:w_]), after=[t_c2], stream="sc")
                cstate["st"] = t_st2
                toks.append(t_st2)

    conv_o, conv_1, conv_2 = [], [], []
    convf(wo_bf, wo, NC, D, conv_o)
    convf(w1_bf, w1, NC, FF1, conv_1)
    convf(w2_bf, w2, nf, D, conv_2)
    sgn_toks = []
    for i in range(NC):
        t_ld = S.add("sync", lambda E, i=i: E.dma_start(cf32[:, 0:D], wsT[i * 128:(i + 1) * 128, :]),
                     after=[cstate["cv"]], stream="lc")
        t_sg = S.add("act", lambda E, i=i: E.activation(sgn[:, i * D:(i + 1) * D], cf32[:, 0:D],
                                                        AF.Sign), after=[t_ld, cstate["st"]])
        cstate["cv"] = t_sg
        sgn_toks.append(t_sg)
    t_b1 = S.add("sync", lambda E: E.dma_start(b1s[:, :], b1v[:, :]), stream="lm")
    t_b2 = S.add("sync", lambda E: E.dma_start(b2s[:, :], b2v[:, :]), stream="lm")
    t_g2 = S.add("sync", lambda E: E.dma_start(g2s[:, :], g2v[:, :]), stream="lm")
    t_be2 = S.add("sync", lambda E: E.dma_start(be2s[:, :], be2v[:, :]), stream="lm")

    def apply_ln(src_sb, dst_sb, tokA, tokB, src_toks, extra=None, extra_toks=None, gb=None):
        toks = []
        prev = [None]
        for c in range(NC):
            s_ap = src_sb[:, c * T:(c + 1) * T]
            d_ap = dst_sb[:, c * T:(c + 1) * T]
            t1 = S.add("dve", lambda E, s_ap=s_ap: E.tensor_tensor(tmp[:, :], s_ap, ln.ps_A[:, :],
                                                                   ALU.mult),
                       after=[tokA, prev[0]] + ([src_toks[c]] if src_toks else []))
            t2 = S.add("dve", lambda E: E.tensor_tensor(tmp[:, :], tmp[:, :], ln.ps_B[:, :],
                                                        ALU.add), after=[tokB, t1])
            if gb is not None:
                gs, bs = gb
                t3 = S.add("dve", lambda E, c=c, d_ap=d_ap, gs=gs, bs=bs: E.tensor_scalar(
                    d_ap, tmp[:, :], gs[:, c:c + 1], bs[:, c:c + 1], ALU.mult, ALU.add),
                    after=[t2, t_g2, t_be2])
            elif extra is not None:
                e_ap = extra[:, c * T:(c + 1) * T]
                t3 = S.add("dve", lambda E, d_ap=d_ap, e_ap=e_ap: E.tensor_tensor(
                    d_ap, tmp[:, :], e_ap, ALU.add),
                    after=[t2] + ([extra_toks[c]] if extra_toks else []))
            else:
                t3 = S.add("dve", lambda E, d_ap=d_ap: E.tensor_copy(d_ap, tmp[:, :]), after=[t2])
            ln.note_use(t3)
            prev[0] = t3
            toks.append(t3)
        return toks

    evac = {0: None, 1: None}
    wf_used = {0: None, 1: None}
    wdb_used = {0: None, 1: None}
    prev_tile = []
    for t in range(nt):
        xin_toks, al_toks, x2_toks = [], [], []
        for c in range(NC):
            xin_toks.append(S.add("sync", lambda E, c=c, t=t: E.dma_start(
                xin[:, c * T:(c + 1) * T], xinT[c * 128:(c + 1) * 128, t * T:(t + 1) * T]),
                after=prev_tile if c == 0 else [], stream="lx"))
            al_toks.append(S.add("sync", lambda E, c=c, t=t: E.dma_start(
                al[:, c * T:(c + 1) * T], alT[c * 128:(c + 1) * 128, t * T:(t + 1) * T]), stream="la"))
            x2_toks.append(S.add("sync", lambda E, c=c, t=t: E.dma_start(
                x2[:, c * T:(c + 1) * T], x2T[c * 128:(c + 1) * 128, t * T:(t + 1) * T]), stream="l2"))
        xmid_toks = []
        for dd in range(NC):
            b = dd % 2
            t_w = S.add("sync", lambda E, dd=dd, b=b: E.dma_start(
                wfb[b][:, :].rearrange("p (c n) -> p c n", c=NC),
                wo_bf[:, dd * 128:(dd + 1) * 128].rearrange("(c p) n -> p c n", p=128)),
                after=[wf_used[b]] + (conv_o if (t == 0 and dd < 2) else []), stream="lw")
            mm = None
            for e in range(NC):
                mm = S.add("pe", lambda E, e=e, b=b, s=(e == 0), st_=(e == NC - 1): E.matmul(
                    ps_m[0][:, :], wfb[b][:, e * 128:(e + 1) * 128], al[:, e * T:(e + 1) * T],
                    start=s, stop=st_),
                    after=[t_w, evac[0], al_toks[e]] if e == 0 else [al_toks[e]])
            wf_used[b] = mm
            t_ev = S.add("dve", lambda E, dd=dd: E.tensor_tensor(
                xmid[:, dd * T:(dd + 1) * T], xin[:, dd * T:(dd + 1) * T], ps_m[0][:, :], ALU.add),
                after=[mm, xin_toks[dd]])
            evac[0] = t_ev
            xmid_toks.append(t_ev)
        tokA, tokB = ln.stats(xmid, xmid_toks)
        h2_toks = apply_ln(xmid, h2, tokA, tokB, xmid_toks, gb=(g2s, be2s))
        a_toks = [None] * nf
        for f in range(nf):
            b = f % 2
            t_w = S.add("sync", lambda E, f=f, b=b: E.dma_start(
                wfb[b][:, :].rearrange("p (c n) -> p c n", c=NC),
                w1_bf[:, f * 128:(f + 1) * 128].rearrange("(c p) n -> p c n", p=128)),
                after=[wf_used[b]] + (conv_1 if (t == 0 and f < 2) else []), stream="lw")
            mm = None
            for e in range(NC):
                mm = S.add("pe", lambda E, e=e, b=b, s=(e == 0), st_=(e == NC - 1): E.matmul(
                    ps_m[0][:, :], wfb[b][:, e * 128:(e + 1) * 128], h2[:, e * T:(e + 1) * T],
                    start=s, stop=st_),
                    after=([t_w, evac[0]] + h2_toks) if e == 0 else [])
            wf_used[b] = mm
            t_a = S.add("act", lambda E, f=f: E.activation(
                aall[:, f * T:(f + 1) * T], ps_m[0][:, :], AF.Gelu, bias=b1s[:, f:f + 1]),
                after=[mm, t_b1])
            evac[0] = t_a
            a_toks[f] = t_a
        xmid2_toks = []
        for dd in range(NC):
            mm = None
            for f in range(nf):
                b = f % 2
                t_w = S.add("sync", lambda E, dd=dd, f=f, b=b: E.dma_start(
                    wdb[b][:, :], w2_bf[f * 128:(f + 1) * 128, dd * 128:(dd + 1) * 128]),
                    after=[wdb_used[b]] + (conv_2 if (t == 0 and dd == 0 and f < 2) else []), stream="lwd")
                mm = S.add("pe", lambda E, f=f, b=b, s=(f == 0), st_=(f == nf - 1): E.matmul(
                    ps_m[1][:, :], wdb[b][:, :], aall[:, f * T:(f + 1) * T], start=s, stop=st_),
                    after=[t_w, a_toks[f]] + ([evac[1]] if f == 0 else []))
                wdb_used[b] = mm
            t_ev = S.add("dve", lambda E, dd=dd: E.scalar_tensor_tensor(
                xmid2[:, dd * T:(dd + 1) * T], ps_m[1][:, :], b2s[:, dd:dd + 1],
                xmid[:, dd * T:(dd + 1) * T], ALU.add, ALU.add),
                after=[mm, t_b2, xmid_toks[dd]])
            evac[1] = t_ev
            xmid2_toks.append(t_ev)
        tokA, tokB = ln.stats(xmid2, xmid2_toks)
        x3_toks = apply_ln(xmid2, x3, tokA, tokB, xmid2_toks, extra=x2, extra_toks=x2_toks)
        tokA, tokB = ln.stats(x3, x3_toks)
        xin2_toks = apply_ln(x3, xin2, tokA, tokB, x3_toks)
        y2_toks = []
        for dd in range(NC):
            mm = None
            for e in range(NC):
                mm = S.add("pe", lambda E, e=e, dd=dd, s=(e == 0), st_=(e == NC - 1): E.matmul(
                    ps_m[0][:, :], sgn[:, e * D + dd * 128: e * D + (dd + 1) * 128],
                    xin2[:, e * T:(e + 1) * T], start=s, stop=st_),
                    after=(sgn_toks + xin2_toks + [evac[0]]) if e == 0 else [])
            t_ev = S.add("act", lambda E, dd=dd: E.activation(
                xmid[:, dd * T:(dd + 1) * T], ps_m[0][:, :], AF.Copy),
                after=[mm, xmid2_toks[dd]])
            evac[0] = t_ev
            y2_toks.append(t_ev)
        tokA, tokB = ln.stats(xmid, y2_toks)
        x4_toks = apply_ln(xmid, xin, tokA, tokB, y2_toks, extra=x3, extra_toks=x3_toks)
        for c in range(NC):
            out_toks.append(S.add("pool", lambda E, c=c, t=t: E.dma_start(
                x4T[c * 128:(c + 1) * 128, t * T:(t + 1) * T], xin[:, c * T:(c + 1) * T]),
                after=[x4_toks[c]], stream="so"))
        prev_tile = S.barrier()

    return _finish(nc, S, st, out_toks)



def _alibi_slope(h):
    x = (2 ** 8) ** (1.0 / H)
    return 1.0 / x ** (h + 1)


def kernel(**inp):
    bf = ml_dtypes.bfloat16
    B, SEQ = 4, 4096
    NTOK = B * SEQ
    TOK = NTOK // NCORES

    x = np.asarray(inp["L0_attn_x"], dtype=np.float32)
    xT = np.ascontiguousarray(x.reshape(NTOK, D).T)
    wkqvT = np.ascontiguousarray(np.asarray(inp["L0_attn_W_kqv"], np.float32).T)
    wgT = np.ascontiguousarray(np.asarray(inp["L0_ffn_w_gate"], np.int32).T)
    wuT = np.ascontiguousarray(np.asarray(inp["L0_ffn_w_up"], np.int32).T)
    wdT = np.ascontiguousarray(np.asarray(inp["L0_ffn_w_down"], np.int32).T)
    wq = np.ascontiguousarray(np.asarray(inp["L1_attn_wq"], np.float32))
    wk = np.ascontiguousarray(np.asarray(inp["L1_attn_wk"], np.float32))
    wv = np.ascontiguousarray(np.asarray(inp["L1_attn_wv"], np.float32))
    wo = np.ascontiguousarray(np.asarray(inp["L1_attn_wo"], np.float32))
    proj = np.ascontiguousarray(np.asarray(inp["L1_attn_proj"], np.float32))
    w1 = np.ascontiguousarray(np.asarray(inp["L1_attn_w1"], np.float32))
    w2 = np.ascontiguousarray(np.asarray(inp["L1_attn_w2"], np.float32))
    b1 = np.asarray(inp["L1_attn_b1"], np.float32)
    b2 = np.asarray(inp["L1_attn_b2"], np.float32)
    g1 = np.asarray(inp["L1_attn_g1"], np.float32)
    be1 = np.asarray(inp["L1_attn_beta1"], np.float32)
    g2 = np.asarray(inp["L1_attn_g2"], np.float32)
    be2 = np.asarray(inp["L1_attn_beta2"], np.float32)
    wsT = np.ascontiguousarray(np.asarray(inp["L1_ffn_weight"], np.float32).T)

    g1v = np.ascontiguousarray(g1.reshape(NC, 128).T)
    b1v_l1 = np.ascontiguousarray(be1.reshape(NC, 128).T)
    b1ffn = np.ascontiguousarray(b1.reshape(32, 128).T)
    b2ffn = np.ascontiguousarray(b2.reshape(NC, 128).T)
    g2v = np.ascontiguousarray(g2.reshape(NC, 128).T)
    be2v = np.ascontiguousarray(be2.reshape(NC, 128).T)

    cols = lambda a, k: np.ascontiguousarray(a[:, k * TOK:(k + 1) * TOK])

    ncA = build_A(TOK)
    in_maps = [{"xT": cols(xT, k), "wkqvT": wkqvT} for k in range(NCORES)]
    resA = run_bass_kernel_spmd(ncA, in_maps, core_ids=list(range(NCORES)))
    kqvT = np.concatenate([np.asarray(r["kqvT"]) for r in resA.results], axis=1)

    ncB = build_B(8)
    in_maps = []
    for k in range(NCORES):
        ks, qs, vs, sl = [], [], [], []
        for j in range(8):
            idx = 8 * k + j
            bb, hh_ = idx // 16, idx % 16
            ks.append(kqvT[hh_ * 64:(hh_ + 1) * 64, bb * SEQ:(bb + 1) * SEQ])
            qs.append(kqvT[D + hh_ * 64:D + (hh_ + 1) * 64, bb * SEQ:(bb + 1) * SEQ])
            vs.append(kqvT[2 * D + hh_ * 64:2 * D + (hh_ + 1) * 64, bb * SEQ:(bb + 1) * SEQ])
            sl.append(_alibi_slope(hh_))
        in_maps.append({"kh": np.ascontiguousarray(np.stack(ks)),
                        "qh": np.ascontiguousarray(np.stack(qs)),
                        "vh": np.ascontiguousarray(np.stack(vs)),
                        "slv": np.ascontiguousarray(
                            np.broadcast_to(np.asarray(sl, np.float32)[None, :], (128, 8)))})
    resB = run_bass_kernel_spmd(ncB, in_maps, core_ids=list(range(NCORES)))
    attnT = np.empty((D, NTOK), np.float32)
    for k in range(NCORES):
        ohk = np.asarray(resB.results[k]["oh"])
        for j in range(8):
            idx = 8 * k + j
            bb, hh_ = idx // 16, idx % 16
            attnT[hh_ * 64:(hh_ + 1) * 64, bb * SEQ:(bb + 1) * SEQ] = ohk[j]

    ncC = build_C(TOK)
    in_maps = [{"xT": cols(xT, k), "atT": cols(attnT, k), "wgT": wgT, "wuT": wuT,
                "wdT": wdT, "wq": wq, "wk": wk, "wv": wv, "proj": proj,
                "g1v": g1v, "b1v": b1v_l1} for k in range(NCORES)]
    resC = run_bass_kernel_spmd(ncC, in_maps, core_ids=list(range(NCORES)))
    x2T = np.concatenate([np.asarray(r["x2T"]) for r in resC.results], axis=1)
    xinT = np.concatenate([np.asarray(r["xinT"]) for r in resC.results], axis=1)
    qpT = np.concatenate([np.asarray(r["qpT"]) for r in resC.results], axis=1)
    kpT = np.concatenate([np.asarray(r["kpT"]) for r in resC.results], axis=1)
    vpT = np.concatenate([np.asarray(r["vpT"]) for r in resC.results], axis=1)

    ncD = build_D(8)
    in_maps = []
    for k in range(NCORES):
        qs, ks, vs = [], [], []
        for j in range(8):
            idx = 8 * k + j
            bb, hh_ = idx // 16, idx % 16
            qs.append(qpT[hh_ * 64:(hh_ + 1) * 64, bb * SEQ:(bb + 1) * SEQ])
            ks.append(kpT[hh_ * 64:(hh_ + 1) * 64, bb * SEQ:(bb + 1) * SEQ])
            vs.append(vpT[hh_ * 64:(hh_ + 1) * 64, bb * SEQ:(bb + 1) * SEQ])
        in_maps.append({"qh": np.ascontiguousarray(np.stack(qs)),
                        "kh": np.ascontiguousarray(np.stack(ks)),
                        "vh": np.ascontiguousarray(np.stack(vs))})
    resD = run_bass_kernel_spmd(ncD, in_maps, core_ids=list(range(NCORES)))
    alT = np.empty((D, NTOK), bf)
    for k in range(NCORES):
        ohk = np.asarray(resD.results[k]["oh"])
        for j in range(8):
            idx = 8 * k + j
            bb, hh_ = idx // 16, idx % 16
            alT[hh_ * 64:(hh_ + 1) * 64, bb * SEQ:(bb + 1) * SEQ] = ohk[j]

    ncE = build_E(TOK)
    in_maps = [{"x2T": cols(x2T, k), "xinT": cols(xinT, k), "alT": cols(alT, k),
                "wo": wo, "w1": w1, "w2": w2, "wsT": wsT, "b1v": b1ffn,
                "b2v": b2ffn, "g2v": g2v, "be2v": be2v} for k in range(NCORES)]
    resE = run_bass_kernel_spmd(ncE, in_maps, core_ids=list(range(NCORES)))
    x4T = np.concatenate([np.asarray(r["x4T"]) for r in resE.results], axis=1)

    return np.ascontiguousarray(x4T.T).reshape(B, SEQ, D).astype(np.float32)


# version 05 (final)
# speedup vs baseline: 1.4387x; 1.4387x over previous
import contextlib

import numpy as np
import ml_dtypes

import concourse.bass as bass
import concourse.mybir as mybir
from concourse.bass_utils import run_bass_kernel_spmd

F32 = mybir.dt.float32
BF16 = mybir.dt.bfloat16
I32 = mybir.dt.int32
AF = mybir.ActivationFunctionType
ALU = mybir.AluOpType

NCORES = 8
D = 1024
NC = 8
H = 16
DH = 64
EPS = 1e-5

ENG = ("sync", "pe", "act", "dve", "pool")


class Sched:

    NSLOT = 4

    def __init__(self):
        self.q = {e: [] for e in ENG}
        self.cnt = {e: 0 for e in ENG}
        self.streams = {}

    def add(self, eng, fn, after=(), stream=None, inc=None):
        aft = tuple(t for t in after if t is not None)
        if stream is None:
            self.q[eng].append(("c", fn, aft, None, 0))
            self.cnt[eng] += 1
            return (eng, self.cnt[eng])
        stt_ = self.streams.setdefault(stream, {"n": 0, "cum": [0] * self.NSLOT})
        slot = stt_["n"] % self.NSLOT
        stt_["n"] += 1
        prev = stt_["cum"][slot]
        stt_["cum"][slot] = prev + 16
        name = "~%s%d" % (stream, slot)
        self.q[eng].append(("d", fn, aft, name, prev))
        return (name, prev + 16)

    def wait_only(self, eng, after):
        aft = tuple(t for t in after if t is not None)
        self.q[eng].append(("w", None, aft, None, 0))

    def barrier(self):
        out = [(e, self.cnt[e]) for e in ENG if self.cnt[e] > 0]
        for s, stt_ in self.streams.items():
            for slot in range(self.NSLOT):
                if stt_["cum"][slot] > 0:
                    out.append(("~%s%d" % (s, slot), stt_["cum"][slot]))
        return out

    def run(self, blk, nc, st):
        sems = {e: st.enter_context(nc.semaphore("sem_" + e)) for e in ENG}
        for s, stt_ in self.streams.items():
            for slot in range(self.NSLOT):
                if stt_["cum"][slot] > 0:
                    nm = "~%s%d" % (s, slot)
                    sems[nm] = st.enter_context(nc.semaphore("sem_s_%s%d" % (s, slot)))

        def runner(ename):
            def go(E):
                waited = {}

                def w(n, v):
                    if waited.get(n, 0) >= v:
                        return
                    E.wait_ge(sems[n], v)
                    waited[n] = v

                for kind, fn, aft, strm, cum in self.q[ename]:
                    for (n, v) in aft:
                        w(n, v)
                    if kind == "w":
                        continue
                    if kind == "d":
                        if cum > 0:
                            w(strm, cum)
                        fn(E).then_inc(sems[strm], 16)
                    else:
                        fn(E).then_inc(sems[ename], 1)
            return go

        if self.q["sync"]:
            blk.sync(runner("sync"))
        if self.q["pe"]:
            blk.tensor(runner("pe"))
        if self.q["act"]:
            blk.scalar(runner("act"))
        if self.q["dve"]:
            blk.vector(runner("dve"))
        if self.q["pool"]:
            blk.gpsimd(runner("pool"))


class LN:

    def __init__(self, S, sb, ps, T):
        self.S = S
        self.T = T
        self.ones_col = sb("ln_ones_col", [128, 1], BF16)
        self.ones_row = sb("ln_ones_row", [1, 128], F32)
        self.xbf = sb("ln_xbf", [128, NC * T], BF16)
        self.sq = sb("ln_sq", [128, NC * T], BF16)
        self.r0 = sb("ln_r0", [1, T], F32)
        self.r1 = sb("ln_r1", [1, T], F32)
        self.r2 = sb("ln_r2", [1, T], F32)
        self.ps_s0 = ps("ln_ps_s0", [1, T], F32)
        self.ps_s1 = ps("ln_ps_s1", [1, T], F32)
        self.ps_A = ps("ln_ps_A", [128, T], F32)
        self.ps_B = ps("ln_ps_B", [128, T], F32)
        t_i1 = S.add("dve", lambda E: E.memset(self.ones_col[:, :], 1.0))
        t_i2 = S.add("dve", lambda E: E.memset(self.ones_row[:, :], 1.0))
        self.free_toks = []
        self.prev_all = [t_i1, t_i2]

    def stats(self, x_sb, x_toks):
        S, T = self.S, self.T
        prev_free = list(self.free_toks)
        pa = list(self.prev_all)
        cast_t, sq_t = [], []
        for c in range(NC):
            xc = x_sb[:, c * T:(c + 1) * T]
            bc = self.xbf[:, c * T:(c + 1) * T]
            sc = self.sq[:, c * T:(c + 1) * T]
            aft = ([x_toks[c]] if x_toks else []) + pa
            cast_t.append(S.add("act", lambda E, bc=bc, xc=xc: E.activation(bc, xc, AF.Copy), after=aft))
            sq_t.append(S.add("act", lambda E, sc=sc, xc=xc: E.activation(sc, xc, AF.Square), after=aft))
        m1 = m2 = None
        for c in range(NC):
            bc = self.xbf[:, c * T:(c + 1) * T]
            sc = self.sq[:, c * T:(c + 1) * T]
            m1 = S.add("pe", lambda E, bc=bc, s=(c == 0), e=(c == NC - 1): E.matmul(
                self.ps_s0[0:1, :], self.ones_col[:, :], bc, start=s, stop=e),
                after=[cast_t[c]] + pa)
            m2 = S.add("pe", lambda E, sc=sc, s=(c == 0), e=(c == NC - 1): E.matmul(
                self.ps_s1[0:1, :], self.ones_col[:, :], sc, start=s, stop=e),
                after=[sq_t[c]])
        t_mean = S.add("dve", lambda E: E.tensor_scalar_mul(self.r0[:, :], self.ps_s0[0:1, :],
                                                            1.0 / D), after=[m1] + pa)
        t_msq = S.add("act", lambda E: E.activation(self.r2[:, :], self.r0[:, :], AF.Square),
                      after=[t_mean] + pa)
        t_var = S.add("dve", lambda E: E.scalar_tensor_tensor(
            self.r1[:, :], self.ps_s1[0:1, :], 1.0 / D, self.r2[:, :], ALU.mult, ALU.subtract),
            after=[m2, t_msq] + pa)
        t_vep = S.add("dve", lambda E: E.tensor_scalar_add(self.r2[:, :], self.r1[:, :], EPS),
                      after=[t_var, t_msq])
        t_lg = S.add("act", lambda E: E.activation(self.r1[:, :], self.r2[:, :], AF.Ln),
                     after=[t_vep])
        t_hf = S.add("dve", lambda E: E.tensor_scalar_mul(self.r2[:, :], self.r1[:, :], -0.5),
                     after=[t_lg])
        t_rstd = S.add("act", lambda E: E.activation(self.r1[:, :], self.r2[:, :], AF.Exp),
                       after=[t_hf])
        t_negm = S.add("dve", lambda E: E.scalar_tensor_tensor(
            self.r2[:, :], self.ps_s0[0:1, :], -1.0 / D, self.r1[:, :], ALU.mult, ALU.mult),
            after=[t_rstd] + pa)
        tokA = S.add("pe", lambda E: E.matmul(self.ps_A[:, :], self.ones_row[0:1, :], self.r1[0:1, :],
                                              start=True, stop=True), after=[t_rstd] + prev_free + pa)
        tokB = S.add("pe", lambda E: E.matmul(self.ps_B[:, :], self.ones_row[0:1, :], self.r2[0:1, :],
                                              start=True, stop=True), after=[t_negm] + pa)
        self.free_toks = []
        self.prev_all = [t_mean, t_msq, t_var, t_vep, t_lg, t_hf, t_rstd, t_negm, tokA, tokB]
        return tokA, tokB

    def note_use(self, tok):
        self.free_toks.append(tok)


def _std_setup(nc):
    S = Sched()
    st = contextlib.ExitStack()
    sb = lambda name, shape, dt: st.enter_context(nc.sbuf_tensor(name, shape, dt))
    ps = lambda name, shape, dt=F32: st.enter_context(nc.psum_tensor(name, shape, dt))
    return S, st, sb, ps


def _finish(nc, S, st, out_dma_toks):
    S.wait_only("pool", out_dma_toks if out_dma_toks else S.barrier())
    blk = st.enter_context(nc.Block())
    S.run(blk, nc, st)
    st.close()
    return nc



def build_A(TOK, T=512):
    nt = TOK // T
    nc = bass.Bass("TRN2", target_bir_lowering=False)
    xT = nc.dram_tensor("xT", [D, TOK], F32, kind="ExternalInput").ap()
    wkqvT = nc.dram_tensor("wkqvT", [D, 3 * D], F32, kind="ExternalInput").ap()
    wk_bf = nc.dram_tensor("wk_bf", [D, 3 * D], BF16, kind="Internal").ap()
    kqvT = nc.dram_tensor("kqvT", [3 * D, TOK], BF16, kind="ExternalOutput").ap()

    S, st, sb, ps = _std_setup(nc)
    ln = LN(S, sb, ps, T)
    x_sb = sb("x_sb", [128, NC * T], F32)
    xn = sb("xn", [128, NC * T], BF16)
    cbuf = [sb("cbuf0", [128, 3 * D], F32), sb("cbuf1", [128, 3 * D], F32)]
    cbf = [sb("cbf0", [128, 3 * D], BF16), sb("cbf1", [128, 3 * D], BF16)]
    wf = [sb("wf0", [128, NC * 128], BF16), sb("wf1", [128, NC * 128], BF16)]
    ob = [sb("ob0", [128, T], BF16), sb("ob1", [128, T], BF16)]
    ps_m = [ps("ps_m0", [128, T], F32), ps("ps_m1", [128, T], F32)]

    out_toks = []
    conv_done = []
    prev = {0: None, 1: None}
    prev_store = {0: None, 1: None}
    for i in range(NC):
        b = i % 2
        t_ld = S.add("sync", lambda E, i=i, b=b: E.dma_start(
            cbuf[b][:, :], wkqvT[i * 128:(i + 1) * 128, :]), after=[prev[b]], stream="lc")
        t_cv = S.add("act", lambda E, b=b: E.activation(cbf[b][:, :], cbuf[b][:, :], AF.Copy),
                     after=[t_ld, prev_store[b]])
        t_stv = S.add("pool", lambda E, i=i, b=b: E.dma_start(
            wk_bf[i * 128:(i + 1) * 128, :], cbf[b][:, :]), after=[t_cv], stream="sc")
        prev[b] = t_cv
        prev_store[b] = t_stv
        conv_done.append(t_stv)

    evac_t = {0: None, 1: None}
    ob_store = {0: None, 1: None}
    wf_used = {0: None, 1: None}
    prev_tile = []
    for t in range(nt):
        x_toks = []
        for c in range(NC):
            x_toks.append(S.add("sync", lambda E, c=c, t=t: E.dma_start(
                x_sb[:, c * T:(c + 1) * T], xT[c * 128:(c + 1) * 128, t * T:(t + 1) * T]),
                after=prev_tile if c == 0 else [], stream="lx"))
        tokA, tokB = ln.stats(x_sb, x_toks)
        xn_toks = []
        for c in range(NC):
            t1 = S.add("dve", lambda E, c=c: E.tensor_tensor(
                x_sb[:, c * T:(c + 1) * T], x_sb[:, c * T:(c + 1) * T], ln.ps_A[:, :], ALU.mult),
                after=[tokA, x_toks[c]])
            t2 = S.add("dve", lambda E, c=c: E.tensor_tensor(
                xn[:, c * T:(c + 1) * T], x_sb[:, c * T:(c + 1) * T], ln.ps_B[:, :], ALU.add),
                after=[tokB, t1])
            xn_toks.append(t2)
            ln.note_use(t2)
        for f in range(24):
            b = f % 2
            t_w = S.add("sync", lambda E, f=f, b=b: E.dma_start(
                wf[b][:, :].rearrange("p (c n) -> p c n", c=NC),
                wk_bf[:, f * 128:(f + 1) * 128].rearrange("(c p) n -> p c n", p=128)),
                after=[wf_used[b]] + (conv_done if (t == 0 and f < 2) else []), stream="lw")
            mm = None
            for e in range(NC):
                mm = S.add("pe", lambda E, e=e, b=b, s=(e == 0), st_=(e == NC - 1): E.matmul(
                    ps_m[b][:, :], wf[b][:, e * 128:(e + 1) * 128], xn[:, e * T:(e + 1) * T],
                    start=s, stop=st_),
                    after=([t_w, evac_t[b]] + xn_toks) if e == 0 else [])
            wf_used[b] = mm
            t_ev = S.add("act", lambda E, b=b: E.activation(ob[b][:, :], ps_m[b][:, :], AF.Copy),
                         after=[mm, ob_store[b]])
            evac_t[b] = t_ev
            t_st = S.add("pool", lambda E, f=f, t=t, b=b: E.dma_start(
                kqvT[f * 128:(f + 1) * 128, t * T:(t + 1) * T], ob[b][:, :]), after=[t_ev], stream="so")
            ob_store[b] = t_st
            out_toks.append(t_st)
        prev_tile = S.barrier()

    return _finish(nc, S, st, out_toks)



def build_B(npair=8, SEQ=4096, T=512):
    nm = SEQ // T
    nchunk = SEQ // 128
    nc = bass.Bass("TRN2", target_bir_lowering=False)
    kh = nc.dram_tensor("kh", [npair, DH, SEQ], BF16, kind="ExternalInput").ap()
    qh = nc.dram_tensor("qh", [npair, DH, SEQ], BF16, kind="ExternalInput").ap()
    vh = nc.dram_tensor("vh", [npair, DH, SEQ], BF16, kind="ExternalInput").ap()
    slv = nc.dram_tensor("slv", [128, npair], F32, kind="ExternalInput").ap()
    oh = nc.dram_tensor("oh", [npair, DH, SEQ], F32, kind="ExternalOutput").ap()

    S, st, sb, ps = _std_setup(nc)
    kT = [sb("kT0", [DH, SEQ], BF16), sb("kT1", [DH, SEQ], BF16)]
    qT = [sb("qT0", [DH, SEQ], BF16), sb("qT1", [DH, SEQ], BF16)]
    vT = [sb("vT0", [DH, SEQ], BF16), sb("vT1", [DH, SEQ], BF16)]
    v_ext = sb("v_ext", [128, 66 * nchunk], BF16)
    idm = sb("idm", [128, 128], BF16)
    ones_row = sb("b_ones_row", [1, 128], F32)
    iota_t = sb("iota_t", [128, T], F32)
    delta_t = sb("delta_t", [128, 32], F32)
    srel = sb("srel", [128, T], F32)
    bias_all = sb("bias_all", [128, 32], F32)
    slopes = sb("slopes", [128, npair], F32)
    u = [sb("u0", [128, T], F32), sb("u1", [128, T], F32)]
    p_bf = [sb("p0", [128, T], BF16), sb("p1", [128, T], BF16)]
    num_sb = sb("num_sb", [DH, T], F32)
    o_sb = sb("o_sb", [DH, T], F32)
    r_den = sb("r_den", [1, T], F32)
    r_rec = sb("r_rec", [1, T], F32)
    ps_s = [ps("ps_s0", [128, T], F32), ps("ps_s1", [128, T], F32)]
    ps_o = ps("ps_o", [65, T], F32)
    ps_tr = [ps("ps_tr0", [128, DH], BF16), ps("ps_tr1", [128, DH], BF16)]
    ps_bc = ps("ps_bc", [DH, T], F32)

    out_toks = []
    fillreg = {}

    def _fill(E):
        if "r" not in fillreg:
            fillreg["r"] = E.alloc_register("fillr")
            E.reg_mov(fillreg["r"], -1e30)
        return fillreg["r"]

    t_ones = S.add("dve", lambda E: E.memset(ones_row[:, :], 1.0))
    t_id0 = S.add("dve", lambda E: E.memset(idm[:, :], 1.0))
    t_id = S.add("pool", lambda E: E.affine_select(
        idm[:, :], idm[:, :], [[-1, 128]], ALU.is_equal, 0.0, base=0, channel_multiplier=1),
        after=[t_id0])
    t_iota = S.add("pool", lambda E: E.iota(
        iota_t[:, :], [[-1, T]], base=0, channel_multiplier=1,
        allow_small_or_imprecise_dtypes=True))
    t_dlt = S.add("pool", lambda E: E.iota(
        delta_t[:, :], [[128, 32]], base=-28 * 128, channel_multiplier=0,
        allow_small_or_imprecise_dtypes=True))
    t_slv = S.add("sync", lambda E: E.dma_start(slopes[:, :], slv[:, :]), stream="lm")
    t_vz = S.add("dve", lambda E: E.memset(v_ext[:, :], 0.0))
    vone_toks = []
    for ci in range(nchunk):
        vone_toks.append(S.add("dve", lambda E, ci=ci: E.memset(
            v_ext[:, ci * 66 + 64: ci * 66 + 65], 1.0), after=[t_vz]))

    ld_gate = {0: [], 1: []}
    for j in range(npair):
        b = j % 2
        pb = S.barrier() if j > 0 else []
        t_lk = S.add("sync", lambda E, j=j, b=b: E.dma_start(kT[b][:, :], kh[j, :, :]),
                     after=ld_gate[b], stream="lk")
        t_lq = S.add("sync", lambda E, j=j, b=b: E.dma_start(qT[b][:, :], qh[j, :, :]), stream="lq")
        t_lv = S.add("sync", lambda E, j=j, b=b: E.dma_start(vT[b][:, :], vh[j, :, :]), stream="lv")
        t_srel = S.add("dve", lambda E, j=j: E.tensor_scalar_mul(
            srel[:, :], iota_t[:, :], slopes[:, j:j + 1]), after=[t_iota, t_slv] + pb)
        t_bias = S.add("dve", lambda E, j=j: E.tensor_scalar_mul(
            bias_all[:, :], delta_t[:, :], slopes[:, j:j + 1]), after=[t_dlt, t_slv] + pb)
        prev_copy = {0: None, 1: None}
        tr_copy_toks = []
        for ci in range(nchunk):
            tb = ci % 2
            t_tr = S.add("pe", lambda E, ci=ci, b=b, tb=tb: E.transpose(
                ps_tr[tb][:, :], vT[b][:, ci * 128:(ci + 1) * 128], idm[0:DH, 0:DH]),
                after=[t_lv, t_id, prev_copy[tb]] + (pb if ci < 2 else []))
            t_cp = S.add("act", lambda E, ci=ci, tb=tb: E.activation(
                v_ext[:, ci * 66: ci * 66 + 64], ps_tr[tb][:, :], AF.Copy),
                after=[t_tr] + (vone_toks if ci == 0 else []) + (pb if ci < 2 else []))
            prev_copy[tb] = t_cp
            tr_copy_toks.append(t_cp)

        stt_done = {0: None, 1: None}
        exp_done = {0: None, 1: None}
        av_done = {0: None, 1: None}
        post_prev = []
        for m in range(nm):
            nch = 4 * m + 4
            t_av = None
            for c in range(nch):
                si = c % 2
                t_mm = S.add("pe", lambda E, c=c, m=m, b=b, si=si: E.matmul(
                    ps_s[si][:, :], kT[b][:, c * 128:(c + 1) * 128], qT[b][:, m * T:(m + 1) * T],
                    start=True, stop=True),
                    after=[t_lk, t_lq, stt_done[si]])
                t_stt = S.add("dve", lambda E, si=si: E.scalar_tensor_tensor(
                    u[si][:, :], ps_s[si][:, :], 1.0 / 32.0, srel[:, :], ALU.mult, ALU.add),
                    after=[t_mm, t_srel, exp_done[si]])
                stt_done[si] = t_stt
                delta = 128 * c - T * m
                idx = (delta // 128) + 28
                t_msk = None
                if c >= 4 * m:
                    t_msk = S.add("pool", lambda E, si=si, delta=delta: E.affine_select(
                        u[si][:, :], u[si][:, :], [[1, T]], ALU.is_ge, _fill(E),
                        base=-delta, channel_multiplier=-1), after=[t_stt])
                t_exp = S.add("act", lambda E, si=si, idx=idx: E.activation(
                    p_bf[si][:, :], u[si][:, :], AF.Exp, bias=bias_all[:, idx:idx + 1]),
                    after=[t_stt, t_msk, t_bias, av_done[si]])
                exp_done[si] = t_exp
                t_av = S.add("pe", lambda E, c=c, si=si, s=(c == 0), e=(c == nch - 1): E.matmul(
                    ps_o[:, :], v_ext[:, c * 66: c * 66 + 65], p_bf[si][:, :], start=s, stop=e),
                    after=[t_exp] + (tr_copy_toks + post_prev if c == 0 else []))
                av_done[si] = t_av
            t_den = S.add("act", lambda E: E.activation(r_den[:, :], ps_o[64:65, :], AF.Copy),
                          after=[t_av] + post_prev)
            t_rec = S.add("dve", lambda E: E.reciprocal(r_rec[:, :], r_den[:, :]),
                          after=[t_den] + post_prev)
            t_bc = S.add("pe", lambda E: E.matmul(ps_bc[:, :], ones_row[0:1, 0:DH], r_rec[0:1, :],
                                                  start=True, stop=True),
                         after=[t_rec, t_ones] + post_prev)
            t_num = S.add("act", lambda E: E.activation(num_sb[:, :], ps_o[0:DH, :], AF.Copy),
                          after=[t_av] + post_prev)
            t_mul = S.add("dve", lambda E: E.tensor_tensor(o_sb[:, :], num_sb[:, :], ps_bc[0:DH, :],
                                                           ALU.mult), after=[t_bc, t_num] + post_prev)
            t_st = S.add("pool", lambda E, j=j, m=m: E.dma_start(
                oh[j, :, m * T:(m + 1) * T], o_sb[:, :]), after=[t_mul], stream="so")
            out_toks.append(t_st)
            post_prev = [t_den, t_rec, t_bc, t_num, t_mul, t_st]
        ld_gate[b] = S.barrier()

    return _finish(nc, S, st, out_toks)



def build_C(TOK, T=512):
    nt = TOK // T
    FF0 = 3072
    nf = FF0 // 128
    nc = bass.Bass("TRN2", target_bir_lowering=False)
    xT = nc.dram_tensor("xT", [D, TOK], F32, kind="ExternalInput").ap()
    atT = nc.dram_tensor("atT", [D, TOK], F32, kind="ExternalInput").ap()
    wgT = nc.dram_tensor("wgT", [D, FF0], I32, kind="ExternalInput").ap()
    wuT = nc.dram_tensor("wuT", [D, FF0], I32, kind="ExternalInput").ap()
    wdT = nc.dram_tensor("wdT", [FF0, D], I32, kind="ExternalInput").ap()
    wq = nc.dram_tensor("wq", [D, D], F32, kind="ExternalInput").ap()
    wk = nc.dram_tensor("wk", [D, D], F32, kind="ExternalInput").ap()
    wv = nc.dram_tensor("wv", [D, D], F32, kind="ExternalInput").ap()
    proj = nc.dram_tensor("proj", [DH, DH], F32, kind="ExternalInput").ap()
    g1v = nc.dram_tensor("g1v", [128, NC], F32, kind="ExternalInput").ap()
    b1v = nc.dram_tensor("b1v", [128, NC], F32, kind="ExternalInput").ap()
    wg_bf = nc.dram_tensor("wg_bf", [D, FF0], BF16, kind="Internal").ap()
    wu_bf = nc.dram_tensor("wu_bf", [D, FF0], BF16, kind="Internal").ap()
    wd_bf = nc.dram_tensor("wd_bf", [FF0, D], BF16, kind="Internal").ap()
    wq_bf = nc.dram_tensor("wq_bf", [D, D], BF16, kind="Internal").ap()
    wk_bf = nc.dram_tensor("wk_bf", [D, D], BF16, kind="Internal").ap()
    wv_bf = nc.dram_tensor("wv_bf", [D, D], BF16, kind="Internal").ap()
    x2T = nc.dram_tensor("x2T", [D, TOK], F32, kind="ExternalOutput").ap()
    xinT = nc.dram_tensor("xinT", [D, TOK], F32, kind="ExternalOutput").ap()
    qpT = nc.dram_tensor("qpT", [D, TOK], BF16, kind="ExternalOutput").ap()
    kpT = nc.dram_tensor("kpT", [D, TOK], BF16, kind="ExternalOutput").ap()
    vpT = nc.dram_tensor("vpT", [D, TOK], BF16, kind="ExternalOutput").ap()

    S, st, sb, ps = _std_setup(nc)
    ln = LN(S, sb, ps, T)
    x_sb = sb("x_sb", [128, NC * T], F32)
    at_sb = sb("at_sb", [128, NC * T], F32)
    x1 = sb("x1", [128, NC * T], F32)
    x2 = sb("x2", [128, NC * T], F32)
    xin = sb("xin", [128, NC * T], F32)
    xn1 = sb("xn1", [128, NC * T], BF16)
    h1 = sb("h1", [128, NC * T], BF16)
    qt = sb("qt", [128, NC * T], BF16)
    kt = sb("kt", [128, NC * T], BF16)
    hh = sb("hh", [128, nf * T], BF16)
    sg = sb("sg", [128, T], F32)
    tmp = sb("tmpc", [128, T], F32)
    CW = 1536
    ci32 = sb("ci32", [128, CW], I32)
    cf32 = sb("cf32", [128, CW], F32)
    cbf = sb("cbf", [128, CW], BF16)
    wfb = [sb("wfb0", [128, NC * 128], BF16), sb("wfb1", [128, NC * 128], BF16)]
    wfb2 = [sb("wfb2_0", [128, NC * 128], BF16), sb("wfb2_1", [128, NC * 128], BF16)]
    wdb = [sb("wdb0", [128, 3072], BF16), sb("wdb1", [128, 3072], BF16)]
    projb = sb("projb", [128, DH], BF16)
    projf = sb("projf", [128, DH], F32)
    g1s = sb("g1s", [128, NC], F32)
    b1s = sb("b1s", [128, NC], F32)
    ph_a = sb("ph_a", [DH, T], F32)
    ph_b = sb("ph_b", [DH, T], F32)
    ph_e = sb("ph_e", [DH, T], F32)
    ob = [sb("obc0", [128, T], BF16), sb("obc1", [128, T], BF16)]
    ps_m = [ps("cps_m0", [128, T], F32), ps("cps_m1", [128, T], F32)]
    ps_phi = ps("ps_phi", [DH, T], F32)

    out_toks = []
    cstate = {"cv": None, "st": None}

    def conv(dst, src, nblk, width, scale, toks, is_int):
        for i in range(nblk):
            for j0 in range(0, width, CW):
                w_ = min(CW, width - j0)
                t_ld = S.add("sync", lambda E, i=i, j0=j0, w_=w_, src=src, is_int=is_int: E.dma_start(
                    (ci32 if is_int else cf32)[:, 0:w_], src[i * 128:(i + 1) * 128, j0:j0 + w_]),
                    after=[cstate["cv"]], stream="lc")
                if is_int:
                    t_c1 = S.add("dve", lambda E, w_=w_: E.tensor_copy(
                        cf32[:, 0:w_], ci32[:, 0:w_]), after=[t_ld, cstate["st"]])
                    t_c2 = S.add("dve", lambda E, w_=w_, scale=scale: E.tensor_scalar_mul(
                        cbf[:, 0:w_], cf32[:, 0:w_], scale), after=[t_c1, cstate["st"]])
                    cstate["cv"] = t_c2
                else:
                    t_c2 = S.add("act", lambda E, w_=w_: E.activation(
                        cbf[:, 0:w_], cf32[:, 0:w_], AF.Copy), after=[t_ld, cstate["st"]])
                    cstate["cv"] = t_c2
                t_st2 = S.add("pool", lambda E, i=i, j0=j0, w_=w_, dst=dst: E.dma_start(
                    dst[i * 128:(i + 1) * 128, j0:j0 + w_], cbf[:, 0:w_]), after=[t_c2], stream="sc")
                cstate["st"] = t_st2
                toks.append(t_st2)

    conv_g, conv_u, conv_d, conv_q, conv_k, conv_v = [], [], [], [], [], []
    conv(wg_bf, wgT, NC, FF0, 1.0 / 127.0, conv_g, True)
    conv(wu_bf, wuT, NC, FF0, 1.0 / 127.0, conv_u, True)
    conv(wd_bf, wdT, nf, D, 1.0 / 127.0, conv_d, True)
    conv(wq_bf, wq, NC, D, 1.0, conv_q, False)
    conv(wk_bf, wk, NC, D, 1.0, conv_k, False)
    conv(wv_bf, wv, NC, D, 1.0, conv_v, False)
    t_pl0 = S.add("sync", lambda E: E.dma_start(projf[0:DH, :], proj[:, :]), stream="lm")
    t_pl1 = S.add("sync", lambda E: E.dma_start(projf[DH:128, :], proj[:, :]), stream="lm")
    t_pb = S.add("act", lambda E: E.activation(projb[:, :], projf[:, :], AF.Copy),
                 after=[t_pl0, t_pl1])
    t_g1 = S.add("sync", lambda E: E.dma_start(g1s[:, :], g1v[:, :]), stream="lm")
    t_b1 = S.add("sync", lambda E: E.dma_start(b1s[:, :], b1v[:, :]), stream="lm")

    def apply_ln(src_sb, dst_sb, tokA, tokB, src_toks, extra=None, extra_toks=None, gb=None):
        toks = []
        prev = [None]
        for c in range(NC):
            s_ap = src_sb[:, c * T:(c + 1) * T]
            d_ap = dst_sb[:, c * T:(c + 1) * T]
            t1 = S.add("dve", lambda E, s_ap=s_ap: E.tensor_tensor(tmp[:, :], s_ap, ln.ps_A[:, :],
                                                                   ALU.mult),
                       after=[tokA, prev[0]] + ([src_toks[c]] if src_toks else []))
            if gb is not None:
                t2 = S.add("dve", lambda E: E.tensor_tensor(tmp[:, :], tmp[:, :], ln.ps_B[:, :],
                                                            ALU.add), after=[tokB, t1])
                gs, bs = gb
                t3 = S.add("dve", lambda E, c=c, d_ap=d_ap, gs=gs, bs=bs: E.tensor_scalar(
                    d_ap, tmp[:, :], gs[:, c:c + 1], bs[:, c:c + 1], ALU.mult, ALU.add),
                    after=[t2, t_g1, t_b1])
            elif extra is not None:
                t2 = S.add("dve", lambda E: E.tensor_tensor(tmp[:, :], tmp[:, :], ln.ps_B[:, :],
                                                            ALU.add), after=[tokB, t1])
                e_ap = extra[:, c * T:(c + 1) * T]
                t3 = S.add("dve", lambda E, d_ap=d_ap, e_ap=e_ap: E.tensor_tensor(
                    d_ap, tmp[:, :], e_ap, ALU.add),
                    after=[t2] + ([extra_toks[c]] if extra_toks else []))
            else:
                t3 = S.add("dve", lambda E, d_ap=d_ap: E.tensor_tensor(
                    d_ap, tmp[:, :], ln.ps_B[:, :], ALU.add), after=[tokB, t1])
            ln.note_use(t3)
            prev[0] = t3
            toks.append(t3)
        return toks

    evac = {0: None, 1: None}
    ob_st = {0: None, 1: None}
    wf_used = {0: None, 1: None}
    wf2_used = {0: None, 1: None}
    wdb_used = {0: None, 1: None}
    hh_toks = [None] * nf
    prev_tile = []
    for t in range(nt):
        x_toks, a_toks = [], []
        for c in range(NC):
            x_toks.append(S.add("sync", lambda E, c=c, t=t: E.dma_start(
                x_sb[:, c * T:(c + 1) * T], xT[c * 128:(c + 1) * 128, t * T:(t + 1) * T]),
                after=prev_tile if c == 0 else [], stream="lx"))
            a_toks.append(S.add("sync", lambda E, c=c, t=t: E.dma_start(
                at_sb[:, c * T:(c + 1) * T], atT[c * 128:(c + 1) * 128, t * T:(t + 1) * T]), stream="la"))
        tokA, tokB = ln.stats(at_sb, a_toks)
        x1_toks = apply_ln(at_sb, x1, tokA, tokB, a_toks, extra=x_sb, extra_toks=x_toks)
        tokA, tokB = ln.stats(x1, x1_toks)
        xn1_toks = apply_ln(x1, xn1, tokA, tokB, x1_toks)
        for f in range(nf):
            b = f % 2
            t_wg = S.add("sync", lambda E, f=f, b=b: E.dma_start(
                wfb[b][:, :].rearrange("p (c n) -> p c n", c=NC),
                wg_bf[:, f * 128:(f + 1) * 128].rearrange("(c p) n -> p c n", p=128)),
                after=[wf_used[b]] + (conv_g if (t == 0 and f < 2) else []), stream="lwg")
            t_wu = S.add("sync", lambda E, f=f, b=b: E.dma_start(
                wfb2[b][:, :].rearrange("p (c n) -> p c n", c=NC),
                wu_bf[:, f * 128:(f + 1) * 128].rearrange("(c p) n -> p c n", p=128)),
                after=[wf2_used[b]] + (conv_u if (t == 0 and f < 2) else []), stream="lwu")
            mmg = mmu = None
            for e in range(NC):
                mmg = S.add("pe", lambda E, e=e, b=b, s=(e == 0), st_=(e == NC - 1): E.matmul(
                    ps_m[0][:, :], wfb[b][:, e * 128:(e + 1) * 128], xn1[:, e * T:(e + 1) * T],
                    start=s, stop=st_),
                    after=([t_wg, evac[0]] + xn1_toks) if e == 0 else [])
            for e in range(NC):
                mmu = S.add("pe", lambda E, e=e, b=b, s=(e == 0), st_=(e == NC - 1): E.matmul(
                    ps_m[1][:, :], wfb2[b][:, e * 128:(e + 1) * 128], xn1[:, e * T:(e + 1) * T],
                    start=s, stop=st_),
                    after=[t_wu, evac[1]] if e == 0 else [])
            wf_used[b] = mmg
            wf2_used[b] = mmu
            t_sg = S.add("act", lambda E: E.activation(sg[:, :], ps_m[0][:, :], AF.Silu),
                         after=[mmg, evac[0], evac[1]])
            evac[0] = t_sg
            t_hh = S.add("dve", lambda E, f=f: E.tensor_tensor(
                hh[:, f * T:(f + 1) * T], sg[:, :], ps_m[1][:, :], ALU.mult), after=[t_sg, mmu])
            evac[1] = t_hh
            hh_toks[f] = t_hh
        dn_toks = []
        for dd in range(NC):
            b = dd % 2
            t_wd = S.add("sync", lambda E, dd=dd, b=b: E.dma_start(
                wdb[b][:, :].rearrange("p (c n) -> p c n", c=nf),
                wd_bf[:, dd * 128:(dd + 1) * 128].rearrange("(c p) n -> p c n", p=128)),
                after=[wdb_used[b]] + (conv_d if (t == 0 and dd < 2) else []), stream="lwd")
            mmd = None
            for f in range(nf):
                mmd = S.add("pe", lambda E, f=f, b=b, s=(f == 0), st_=(f == nf - 1): E.matmul(
                    ps_m[0][:, :], wdb[b][:, f * 128:(f + 1) * 128], hh[:, f * T:(f + 1) * T],
                    start=s, stop=st_),
                    after=[t_wd, hh_toks[f]] + ([evac[0]] if f == 0 else []))
            wdb_used[b] = mmd
            t_ev = S.add("act", lambda E, dd=dd: E.activation(
                at_sb[:, dd * T:(dd + 1) * T], ps_m[0][:, :], AF.Copy), after=[mmd])
            evac[0] = t_ev
            dn_toks.append(t_ev)
        tokA, tokB = ln.stats(at_sb, dn_toks)
        x2_toks = apply_ln(at_sb, x2, tokA, tokB, dn_toks, extra=x1, extra_toks=x1_toks)
        for c in range(NC):
            out_toks.append(S.add("pool", lambda E, c=c, t=t: E.dma_start(
                x2T[c * 128:(c + 1) * 128, t * T:(t + 1) * T], x2[:, c * T:(c + 1) * T]),
                after=[x2_toks[c]], stream="sx2"))
        tokA, tokB = ln.stats(x2, x2_toks)
        xin_toks = apply_ln(x2, xin, tokA, tokB, x2_toks)
        for c in range(NC):
            out_toks.append(S.add("pool", lambda E, c=c, t=t: E.dma_start(
                xinT[c * 128:(c + 1) * 128, t * T:(t + 1) * T], xin[:, c * T:(c + 1) * T]),
                after=[xin_toks[c]], stream="sxi"))
        tokA, tokB = ln.stats(xin, xin_toks)
        h1_toks = apply_ln(xin, h1, tokA, tokB, xin_toks, gb=(g1s, b1s))
        for wsrc, wconv, kind in ((wq_bf, conv_q, "q"), (wk_bf, conv_k, "k"), (wv_bf, conv_v, "v")):
            for dd in range(NC):
                b = dd % 2
                t_w = S.add("sync", lambda E, dd=dd, b=b, wsrc=wsrc: E.dma_start(
                    wfb[b][:, :].rearrange("p (c n) -> p c n", c=NC),
                    wsrc[:, dd * 128:(dd + 1) * 128].rearrange("(c p) n -> p c n", p=128)),
                    after=[wf_used[b]] + (wconv if (t == 0 and dd < 2) else []), stream="lw")
                mm = None
                for e in range(NC):
                    mm = S.add("pe", lambda E, e=e, b=b, s=(e == 0), st_=(e == NC - 1): E.matmul(
                        ps_m[0][:, :], wfb[b][:, e * 128:(e + 1) * 128], h1[:, e * T:(e + 1) * T],
                        start=s, stop=st_),
                        after=([t_w, evac[0]] + h1_toks) if e == 0 else [])
                wf_used[b] = mm
                bo = dd % 2
                if kind == "v":
                    t_ev = S.add("act", lambda E, bo=bo: E.activation(ob[bo][:, :], ps_m[0][:, :],
                                                                      AF.Copy),
                                 after=[mm, ob_st[bo]])
                    evac[0] = t_ev
                    t_s = S.add("pool", lambda E, dd=dd, t=t, bo=bo: E.dma_start(
                        vpT[dd * 128:(dd + 1) * 128, t * T:(t + 1) * T], ob[bo][:, :]),
                        after=[t_ev], stream="svp")
                    ob_st[bo] = t_s
                    out_toks.append(t_s)
                else:
                    dst = qt if kind == "q" else kt
                    t_ev = S.add("act", lambda E, dd=dd, dst=dst: E.activation(
                        dst[:, dd * T:(dd + 1) * T], ps_m[0][:, :], AF.Copy), after=[mm])
                    evac[0] = t_ev
        bar = S.barrier()
        phi_prev = []
        for src_sb, dst_dram in ((qt, qpT), (kt, kpT)):
            for hd in range(H):
                cchunk = hd // 2
                poff = (hd % 2) * DH
                rhs = src_sb[poff:poff + DH, cchunk * T:(cchunk + 1) * T]
                mm = S.add("pe", lambda E, rhs=rhs, poff=poff: E.matmul(
                    ps_phi[:, :], projb[poff:poff + DH, :], rhs, start=True, stop=True),
                    after=bar + [t_pb] + phi_prev)
                t_a = S.add("dve", lambda E: E.tensor_scalar_min(ph_a[:, :], ps_phi[:, :], 0.0),
                            after=[mm] + phi_prev)
                t_b2 = S.add("dve", lambda E: E.tensor_scalar_max(ph_b[:, :], ps_phi[:, :], 0.0),
                             after=[mm])
                t_e = S.add("act", lambda E: E.activation(ph_e[:, :], ph_a[:, :], AF.Exp),
                            after=[t_a] + phi_prev)
                bo = hd % 2
                t_o = S.add("dve", lambda E, bo=bo: E.tensor_tensor(
                    ob[bo][0:DH, :], ph_e[:, :], ph_b[:, :], ALU.add),
                    after=[t_e, t_b2, ob_st[bo]])
                t_s = S.add("pool", lambda E, hd=hd, t=t, bo=bo, dst_dram=dst_dram: E.dma_start(
                    dst_dram[hd * DH:(hd + 1) * DH, t * T:(t + 1) * T], ob[bo][0:DH, :]),
                    after=[t_o], stream="sqp")
                ob_st[bo] = t_s
                phi_prev = [t_a, t_b2, t_o, t_e]
                out_toks.append(t_s)
        prev_tile = S.barrier()

    return _finish(nc, S, st, out_toks)



def build_D(npair=8, SEQ=4096):
    CH = 128
    nchunk = SEQ // CH
    nc = bass.Bass("TRN2", target_bir_lowering=False)
    qh = nc.dram_tensor("qh", [npair, DH, SEQ], BF16, kind="ExternalInput").ap()
    kh = nc.dram_tensor("kh", [npair, DH, SEQ], BF16, kind="ExternalInput").ap()
    vh = nc.dram_tensor("vh", [npair, DH, SEQ], BF16, kind="ExternalInput").ap()
    oh = nc.dram_tensor("oh", [npair, DH, SEQ], BF16, kind="ExternalOutput").ap()

    S, st, sb, ps = _std_setup(nc)
    qT = [sb("qT0", [DH, SEQ], BF16), sb("qT1", [DH, SEQ], BF16)]
    kT = [sb("kT0", [DH, SEQ], BF16), sb("kT1", [DH, SEQ], BF16)]
    vT = [sb("vT0", [DH, SEQ], BF16), sb("vT1", [DH, SEQ], BF16)]
    o_all = sb("o_all", [DH, SEQ], BF16)
    idm = sb("idm", [128, 128], BF16)
    maskt = sb("maskt", [128, CH], F32)
    ones_row = sb("d_ones_row", [1, 128], F32)
    v_ext = sb("v_ext", [128, DH + 1], BF16)
    k_tok = sb("k_tok", [128, DH], BF16)
    p_lin = sb("p_lin", [128, CH], BF16)
    kv_f = sb("kv_f", [DH, DH + 1], F32)
    kv_b = sb("kv_b", [DH, DH + 1], BF16)
    num_sb = sb("num_sb", [DH, CH], F32)
    r_den = sb("r_den", [1, CH], F32)
    r_rec = sb("r_rec", [1, CH], F32)
    ps_trv = ps("ps_trv", [128, DH], BF16)
    ps_trk = ps("ps_trk", [128, DH], BF16)
    ps_sl = ps("ps_sl", [128, CH], F32)
    ps_o = ps("ps_o", [DH + 1, CH], F32)
    ps_kv = ps("ps_kv", [DH, DH + 1], F32)
    ps_bc = ps("ps_bc", [DH, CH], F32)

    out_toks = []
    t_ones = S.add("dve", lambda E: E.memset(ones_row[:, :], 1.0))
    t_id0 = S.add("dve", lambda E: E.memset(idm[:, :], 1.0))
    t_id = S.add("pool", lambda E: E.affine_select(
        idm[:, :], idm[:, :], [[-1, 128]], ALU.is_equal, 0.0, base=0, channel_multiplier=1),
        after=[t_id0])
    t_m0 = S.add("dve", lambda E: E.memset(maskt[:, :], 1.0))
    t_mask = S.add("pool", lambda E: E.affine_select(
        maskt[:, :], maskt[:, :], [[1, CH]], ALU.is_ge, 0.0, base=0, channel_multiplier=-1),
        after=[t_m0])
    t_vone = S.add("dve", lambda E: E.memset(v_ext[:, DH:DH + 1], 1.0))

    ld_gate = {0: [], 1: []}
    last_store = None
    for j in range(npair):
        b = j % 2
        pb = S.barrier() if j > 0 else []
        t_lq = S.add("sync", lambda E, j=j, b=b: E.dma_start(qT[b][:, :], qh[j, :, :]),
                     after=ld_gate[b], stream="lq")
        t_lk = S.add("sync", lambda E, j=j, b=b: E.dma_start(kT[b][:, :], kh[j, :, :]), stream="lk")
        t_lv = S.add("sync", lambda E, j=j, b=b: E.dma_start(vT[b][:, :], vh[j, :, :]), stream="lv")
        t_z1 = S.add("dve", lambda E: E.memset(kv_f[:, :], 0.0), after=pb)
        t_kvb = S.add("act", lambda E: E.activation(kv_b[:, :], kv_f[:, :], AF.Copy),
                      after=[t_z1] + pb)
        t_kvadd = t_z1

        cpv_gate = pb
        cpk_gate = pb
        sl_gate = pb
        o_gate = pb
        for i in range(nchunk):
            sl = slice(i * CH, (i + 1) * CH)
            t_trv = S.add("pe", lambda E, b=b, sl=sl: E.transpose(
                ps_trv[:, :], vT[b][:, sl], idm[0:DH, 0:DH]),
                after=[t_lv, t_id] + cpv_gate)
            t_cpv = S.add("act", lambda E: E.activation(v_ext[:, 0:DH], ps_trv[:, :], AF.Copy),
                          after=[t_trv, t_vone] + cpv_gate)
            t_trk = S.add("pe", lambda E, b=b, sl=sl: E.transpose(
                ps_trk[:, :], kT[b][:, sl], idm[0:DH, 0:DH]), after=[t_lk, t_id] + cpk_gate)
            t_cpk = S.add("act", lambda E: E.activation(k_tok[:, :], ps_trk[:, :], AF.Copy),
                          after=[t_trk] + cpk_gate)
            t_sl = S.add("pe", lambda E, b=b, sl=sl: E.matmul(
                ps_sl[:, :], kT[b][:, sl], qT[b][:, sl], start=True, stop=True),
                after=[t_lk, t_lq] + sl_gate)
            t_pl = S.add("dve", lambda E: E.tensor_tensor(p_lin[:, :], ps_sl[:, :], maskt[:, :],
                                                          ALU.mult),
                         after=[t_sl, t_mask] + sl_gate)
            sl_gate = []
            t_o1 = S.add("pe", lambda E: E.matmul(ps_o[:, :], v_ext[:, :], p_lin[:, :],
                                                  start=True, stop=False),
                         after=[t_pl, t_cpv] + o_gate)
            t_o2 = S.add("pe", lambda E, b=b, sl=sl: E.matmul(
                ps_o[:, :], kv_b[:, :], qT[b][:, sl], start=False, stop=True),
                after=[t_o1, t_kvb])
            t_kv = S.add("pe", lambda E: E.matmul(ps_kv[:, :], k_tok[:, :], v_ext[:, :],
                                                  start=True, stop=True),
                         after=[t_cpk, t_cpv, t_kvadd])
            t_kvadd = S.add("dve", lambda E: E.tensor_tensor(kv_f[:, :], kv_f[:, :], ps_kv[:, :],
                                                             ALU.add), after=[t_kv, t_kvb])
            t_kvb = S.add("act", lambda E: E.activation(kv_b[:, :], kv_f[:, :], AF.Copy),
                          after=[t_kvadd, t_o2])
            cpv_gate = [t_o1, t_kv]
            cpk_gate = [t_kv]
            t_den = S.add("dve", lambda E: E.tensor_scalar_add(
                r_den[:, :], ps_o[DH:DH + 1, :], 1e-6), after=[t_o2] + o_gate)
            t_rec = S.add("dve", lambda E: E.reciprocal(r_rec[:, :], r_den[:, :]),
                          after=[t_den] + o_gate)
            t_bc = S.add("pe", lambda E: E.matmul(ps_bc[:, :], ones_row[0:1, 0:DH], r_rec[0:1, :],
                                                  start=True, stop=True),
                         after=[t_rec, t_ones] + o_gate)
            t_num = S.add("act", lambda E: E.activation(num_sb[:, :], ps_o[0:DH, :], AF.Copy),
                          after=[t_o2] + o_gate)
            t_mul = S.add("dve", lambda E, sl=sl: E.tensor_tensor(
                o_all[:, sl], num_sb[:, :], ps_bc[:, :], ALU.mult),
                after=[t_bc, t_num, last_store])
            sl_gate = [t_pl]
            o_gate = [t_den, t_rec, t_bc, t_num, t_mul]
        t_st = S.add("pool", lambda E, j=j: E.dma_start(oh[j, :, :], o_all[:, :]),
                     after=o_gate, stream="so")
        last_store = t_st
        out_toks.append(t_st)
        ld_gate[b] = S.barrier()

    return _finish(nc, S, st, out_toks)



def build_E(TOK, T=512):
    nt = TOK // T
    FF1 = 4096
    nf = FF1 // 128
    nc = bass.Bass("TRN2", target_bir_lowering=False)
    x2T = nc.dram_tensor("x2T", [D, TOK], F32, kind="ExternalInput").ap()
    xinT = nc.dram_tensor("xinT", [D, TOK], F32, kind="ExternalInput").ap()
    alT = nc.dram_tensor("alT", [D, TOK], BF16, kind="ExternalInput").ap()
    wo = nc.dram_tensor("wo", [D, D], F32, kind="ExternalInput").ap()
    w1 = nc.dram_tensor("w1", [D, FF1], F32, kind="ExternalInput").ap()
    w2 = nc.dram_tensor("w2", [FF1, D], F32, kind="ExternalInput").ap()
    wsT = nc.dram_tensor("wsT", [D, D], F32, kind="ExternalInput").ap()
    b1v = nc.dram_tensor("b1v", [128, nf], F32, kind="ExternalInput").ap()
    b2v = nc.dram_tensor("b2v", [128, NC], F32, kind="ExternalInput").ap()
    g2v = nc.dram_tensor("g2v", [128, NC], F32, kind="ExternalInput").ap()
    be2v = nc.dram_tensor("be2v", [128, NC], F32, kind="ExternalInput").ap()
    wo_bf = nc.dram_tensor("wo_bf", [D, D], BF16, kind="Internal").ap()
    w1_bf = nc.dram_tensor("w1_bf", [D, FF1], BF16, kind="Internal").ap()
    w2_bf = nc.dram_tensor("w2_bf", [FF1, D], BF16, kind="Internal").ap()
    x4T = nc.dram_tensor("x4T", [D, TOK], F32, kind="ExternalOutput").ap()

    S, st, sb, ps = _std_setup(nc)
    ln = LN(S, sb, ps, T)
    xin = sb("xin", [128, NC * T], F32)
    al = sb("al", [128, NC * T], BF16)
    x2 = sb("x2", [128, NC * T], F32)
    xmid = sb("xmid", [128, NC * T], F32)
    xmid2 = sb("xmid2", [128, NC * T], F32)
    h2 = None
    x3 = sb("x3", [128, NC * T], F32)
    xin2 = sb("xin2", [128, NC * T], BF16)
    aall = sb("aall", [128, nf * T], BF16)
    sgn = sb("sgn", [128, NC * D], BF16)
    tmp = sb("tmpe", [128, T], F32)
    CW = 2048
    cf32 = sb("cf32", [128, CW], F32)
    cbf = sb("cbf", [128, CW], BF16)
    wfb = [sb("wfb0", [128, NC * 128], BF16), sb("wfb1", [128, NC * 128], BF16)]
    wdb = [sb("wdb0", [128, 4096], BF16), sb("wdb1", [128, 4096], BF16)]
    b1s = sb("b1s", [128, nf], F32)
    b2s = sb("b2s", [128, NC], F32)
    h2 = al
    g2s = sb("g2s", [128, NC], F32)
    be2s = sb("be2s", [128, NC], F32)
    ps_m = [ps("eps_m0", [128, T], F32), ps("eps_m1", [128, T], F32)]

    out_toks = []
    cstate = {"cv": None, "st": None}

    def convf(dst, src, nblk, width, toks):
        for i in range(nblk):
            for j0 in range(0, width, CW):
                w_ = min(CW, width - j0)
                t_ld = S.add("sync", lambda E, i=i, j0=j0, w_=w_, src=src: E.dma_start(
                    cf32[:, 0:w_], src[i * 128:(i + 1) * 128, j0:j0 + w_]),
                    after=[cstate["cv"]], stream="lc")
                t_c2 = S.add("act", lambda E, w_=w_: E.activation(
                    cbf[:, 0:w_], cf32[:, 0:w_], AF.Copy), after=[t_ld, cstate["st"]])
                cstate["cv"] = t_c2
                t_st2 = S.add("pool", lambda E, i=i, j0=j0, w_=w_, dst=dst: E.dma_start(
                    dst[i * 128:(i + 1) * 128, j0:j0 + w_], cbf[:, 0:w_]), after=[t_c2], stream="sc")
                cstate["st"] = t_st2
                toks.append(t_st2)

    conv_o, conv_1, conv_2 = [], [], []
    convf(wo_bf, wo, NC, D, conv_o)
    convf(w1_bf, w1, NC, FF1, conv_1)
    convf(w2_bf, w2, nf, D, conv_2)
    sgn_toks = []
    for i in range(NC):
        t_ld = S.add("sync", lambda E, i=i: E.dma_start(cf32[:, 0:D], wsT[i * 128:(i + 1) * 128, :]),
                     after=[cstate["cv"]], stream="lc")
        t_sg = S.add("act", lambda E, i=i: E.activation(sgn[:, i * D:(i + 1) * D], cf32[:, 0:D],
                                                        AF.Sign), after=[t_ld, cstate["st"]])
        cstate["cv"] = t_sg
        sgn_toks.append(t_sg)
    t_b1 = S.add("sync", lambda E: E.dma_start(b1s[:, :], b1v[:, :]), stream="lm")
    t_b2 = S.add("sync", lambda E: E.dma_start(b2s[:, :], b2v[:, :]), stream="lm")
    t_g2 = S.add("sync", lambda E: E.dma_start(g2s[:, :], g2v[:, :]), stream="lm")
    t_be2 = S.add("sync", lambda E: E.dma_start(be2s[:, :], be2v[:, :]), stream="lm")

    def apply_ln(src_sb, dst_sb, tokA, tokB, src_toks, extra=None, extra_toks=None, gb=None):
        toks = []
        prev = [None]
        for c in range(NC):
            s_ap = src_sb[:, c * T:(c + 1) * T]
            d_ap = dst_sb[:, c * T:(c + 1) * T]
            t1 = S.add("dve", lambda E, s_ap=s_ap: E.tensor_tensor(tmp[:, :], s_ap, ln.ps_A[:, :],
                                                                   ALU.mult),
                       after=[tokA, prev[0]] + ([src_toks[c]] if src_toks else []))
            t2 = S.add("dve", lambda E: E.tensor_tensor(tmp[:, :], tmp[:, :], ln.ps_B[:, :],
                                                        ALU.add), after=[tokB, t1])
            if gb is not None:
                gs, bs = gb
                t3 = S.add("dve", lambda E, c=c, d_ap=d_ap, gs=gs, bs=bs: E.tensor_scalar(
                    d_ap, tmp[:, :], gs[:, c:c + 1], bs[:, c:c + 1], ALU.mult, ALU.add),
                    after=[t2, t_g2, t_be2])
            elif extra is not None:
                e_ap = extra[:, c * T:(c + 1) * T]
                t3 = S.add("dve", lambda E, d_ap=d_ap, e_ap=e_ap: E.tensor_tensor(
                    d_ap, tmp[:, :], e_ap, ALU.add),
                    after=[t2] + ([extra_toks[c]] if extra_toks else []))
            else:
                t3 = S.add("dve", lambda E, d_ap=d_ap: E.tensor_copy(d_ap, tmp[:, :]), after=[t2])
            ln.note_use(t3)
            prev[0] = t3
            toks.append(t3)
        return toks

    evac = {0: None, 1: None}
    wf_used = {0: None, 1: None}
    wdb_used = {0: None, 1: None}
    prev_tile = []
    for t in range(nt):
        xin_toks, al_toks, x2_toks = [], [], []
        for c in range(NC):
            xin_toks.append(S.add("sync", lambda E, c=c, t=t: E.dma_start(
                xin[:, c * T:(c + 1) * T], xinT[c * 128:(c + 1) * 128, t * T:(t + 1) * T]),
                after=prev_tile if c == 0 else [], stream="lx"))
            al_toks.append(S.add("sync", lambda E, c=c, t=t: E.dma_start(
                al[:, c * T:(c + 1) * T], alT[c * 128:(c + 1) * 128, t * T:(t + 1) * T]), stream="la"))
            x2_toks.append(S.add("sync", lambda E, c=c, t=t: E.dma_start(
                x2[:, c * T:(c + 1) * T], x2T[c * 128:(c + 1) * 128, t * T:(t + 1) * T]), stream="l2"))
        xmid_toks = []
        for dd in range(NC):
            b = dd % 2
            t_w = S.add("sync", lambda E, dd=dd, b=b: E.dma_start(
                wfb[b][:, :].rearrange("p (c n) -> p c n", c=NC),
                wo_bf[:, dd * 128:(dd + 1) * 128].rearrange("(c p) n -> p c n", p=128)),
                after=[wf_used[b]] + (conv_o if (t == 0 and dd < 2) else []), stream="lw")
            mm = None
            for e in range(NC):
                mm = S.add("pe", lambda E, e=e, b=b, s=(e == 0), st_=(e == NC - 1): E.matmul(
                    ps_m[0][:, :], wfb[b][:, e * 128:(e + 1) * 128], al[:, e * T:(e + 1) * T],
                    start=s, stop=st_),
                    after=[t_w, evac[0], al_toks[e]] if e == 0 else [al_toks[e]])
            wf_used[b] = mm
            t_ev = S.add("dve", lambda E, dd=dd: E.tensor_tensor(
                xmid[:, dd * T:(dd + 1) * T], xin[:, dd * T:(dd + 1) * T], ps_m[0][:, :], ALU.add),
                after=[mm, xin_toks[dd]])
            evac[0] = t_ev
            xmid_toks.append(t_ev)
        tokA, tokB = ln.stats(xmid, xmid_toks)
        h2_toks = apply_ln(xmid, h2, tokA, tokB, xmid_toks, gb=(g2s, be2s))
        a_toks = [None] * nf
        for f in range(nf):
            b = f % 2
            t_w = S.add("sync", lambda E, f=f, b=b: E.dma_start(
                wfb[b][:, :].rearrange("p (c n) -> p c n", c=NC),
                w1_bf[:, f * 128:(f + 1) * 128].rearrange("(c p) n -> p c n", p=128)),
                after=[wf_used[b]] + (conv_1 if (t == 0 and f < 2) else []), stream="lw")
            mm = None
            for e in range(NC):
                mm = S.add("pe", lambda E, e=e, b=b, s=(e == 0), st_=(e == NC - 1): E.matmul(
                    ps_m[0][:, :], wfb[b][:, e * 128:(e + 1) * 128], h2[:, e * T:(e + 1) * T],
                    start=s, stop=st_),
                    after=([t_w, evac[0]] + h2_toks) if e == 0 else [])
            wf_used[b] = mm
            t_a = S.add("act", lambda E, f=f: E.activation(
                aall[:, f * T:(f + 1) * T], ps_m[0][:, :], AF.Gelu, bias=b1s[:, f:f + 1]),
                after=[mm, t_b1])
            evac[0] = t_a
            a_toks[f] = t_a
        xmid2_toks = []
        for dd in range(NC):
            b = dd % 2
            t_w = S.add("sync", lambda E, dd=dd, b=b: E.dma_start(
                wdb[b][:, :].rearrange("p (c n) -> p c n", c=nf),
                w2_bf[:, dd * 128:(dd + 1) * 128].rearrange("(c p) n -> p c n", p=128)),
                after=[wdb_used[b]] + (conv_2 if (t == 0 and dd < 2) else []), stream="lwd")
            mm = None
            for f in range(nf):
                mm = S.add("pe", lambda E, f=f, b=b, s=(f == 0), st_=(f == nf - 1): E.matmul(
                    ps_m[1][:, :], wdb[b][:, f * 128:(f + 1) * 128], aall[:, f * T:(f + 1) * T],
                    start=s, stop=st_),
                    after=[t_w, a_toks[f]] + ([evac[1]] if f == 0 else []))
            wdb_used[b] = mm
            t_ev = S.add("dve", lambda E, dd=dd: E.scalar_tensor_tensor(
                xmid2[:, dd * T:(dd + 1) * T], ps_m[1][:, :], b2s[:, dd:dd + 1],
                xmid[:, dd * T:(dd + 1) * T], ALU.add, ALU.add),
                after=[mm, t_b2, xmid_toks[dd]])
            evac[1] = t_ev
            xmid2_toks.append(t_ev)
        tokA, tokB = ln.stats(xmid2, xmid2_toks)
        x3_toks = apply_ln(xmid2, x3, tokA, tokB, xmid2_toks, extra=x2, extra_toks=x2_toks)
        tokA, tokB = ln.stats(x3, x3_toks)
        xin2_toks = apply_ln(x3, xin2, tokA, tokB, x3_toks)
        y2_toks = []
        for dd in range(NC):
            mm = None
            for e in range(NC):
                mm = S.add("pe", lambda E, e=e, dd=dd, s=(e == 0), st_=(e == NC - 1): E.matmul(
                    ps_m[0][:, :], sgn[:, e * D + dd * 128: e * D + (dd + 1) * 128],
                    xin2[:, e * T:(e + 1) * T], start=s, stop=st_),
                    after=(sgn_toks + xin2_toks + [evac[0]]) if e == 0 else [])
            t_ev = S.add("act", lambda E, dd=dd: E.activation(
                xmid[:, dd * T:(dd + 1) * T], ps_m[0][:, :], AF.Copy),
                after=[mm, xmid2_toks[dd]])
            evac[0] = t_ev
            y2_toks.append(t_ev)
        tokA, tokB = ln.stats(xmid, y2_toks)
        x4_toks = apply_ln(xmid, xin, tokA, tokB, y2_toks, extra=x3, extra_toks=x3_toks)
        for c in range(NC):
            out_toks.append(S.add("pool", lambda E, c=c, t=t: E.dma_start(
                x4T[c * 128:(c + 1) * 128, t * T:(t + 1) * T], xin[:, c * T:(c + 1) * T]),
                after=[x4_toks[c]], stream="so"))
        prev_tile = S.barrier()

    return _finish(nc, S, st, out_toks)



def _alibi_slope(h):
    x = (2 ** 8) ** (1.0 / H)
    return 1.0 / x ** (h + 1)


def kernel(**inp):
    bf = ml_dtypes.bfloat16
    B, SEQ = 4, 4096
    NTOK = B * SEQ
    TOK = NTOK // NCORES

    x = np.asarray(inp["L0_attn_x"], dtype=np.float32)
    xT = np.ascontiguousarray(x.reshape(NTOK, D).T)
    wkqvT = np.ascontiguousarray(np.asarray(inp["L0_attn_W_kqv"], np.float32).T)
    wgT = np.ascontiguousarray(np.asarray(inp["L0_ffn_w_gate"], np.int32).T)
    wuT = np.ascontiguousarray(np.asarray(inp["L0_ffn_w_up"], np.int32).T)
    wdT = np.ascontiguousarray(np.asarray(inp["L0_ffn_w_down"], np.int32).T)
    wq = np.ascontiguousarray(np.asarray(inp["L1_attn_wq"], np.float32))
    wk = np.ascontiguousarray(np.asarray(inp["L1_attn_wk"], np.float32))
    wv = np.ascontiguousarray(np.asarray(inp["L1_attn_wv"], np.float32))
    wo = np.ascontiguousarray(np.asarray(inp["L1_attn_wo"], np.float32))
    proj = np.ascontiguousarray(np.asarray(inp["L1_attn_proj"], np.float32))
    w1 = np.ascontiguousarray(np.asarray(inp["L1_attn_w1"], np.float32))
    w2 = np.ascontiguousarray(np.asarray(inp["L1_attn_w2"], np.float32))
    b1 = np.asarray(inp["L1_attn_b1"], np.float32)
    b2 = np.asarray(inp["L1_attn_b2"], np.float32)
    g1 = np.asarray(inp["L1_attn_g1"], np.float32)
    be1 = np.asarray(inp["L1_attn_beta1"], np.float32)
    g2 = np.asarray(inp["L1_attn_g2"], np.float32)
    be2 = np.asarray(inp["L1_attn_beta2"], np.float32)
    wsT = np.ascontiguousarray(np.asarray(inp["L1_ffn_weight"], np.float32).T)

    g1v = np.ascontiguousarray(g1.reshape(NC, 128).T)
    b1v_l1 = np.ascontiguousarray(be1.reshape(NC, 128).T)
    b1ffn = np.ascontiguousarray(b1.reshape(32, 128).T)
    b2ffn = np.ascontiguousarray(b2.reshape(NC, 128).T)
    g2v = np.ascontiguousarray(g2.reshape(NC, 128).T)
    be2v = np.ascontiguousarray(be2.reshape(NC, 128).T)

    cols = lambda a, k: np.ascontiguousarray(a[:, k * TOK:(k + 1) * TOK])

    ncA = build_A(TOK)
    in_maps = [{"xT": cols(xT, k), "wkqvT": wkqvT} for k in range(NCORES)]
    resA = run_bass_kernel_spmd(ncA, in_maps, core_ids=list(range(NCORES)))
    kqvT = np.concatenate([np.asarray(r["kqvT"]) for r in resA.results], axis=1)

    ncB = build_B(8)
    in_maps = []
    for k in range(NCORES):
        ks, qs, vs, sl = [], [], [], []
        for j in range(8):
            idx = 8 * k + j
            bb, hh_ = idx // 16, idx % 16
            ks.append(kqvT[hh_ * 64:(hh_ + 1) * 64, bb * SEQ:(bb + 1) * SEQ])
            qs.append(kqvT[D + hh_ * 64:D + (hh_ + 1) * 64, bb * SEQ:(bb + 1) * SEQ])
            vs.append(kqvT[2 * D + hh_ * 64:2 * D + (hh_ + 1) * 64, bb * SEQ:(bb + 1) * SEQ])
            sl.append(_alibi_slope(hh_))
        in_maps.append({"kh": np.ascontiguousarray(np.stack(ks)),
                        "qh": np.ascontiguousarray(np.stack(qs)),
                        "vh": np.ascontiguousarray(np.stack(vs)),
                        "slv": np.ascontiguousarray(
                            np.broadcast_to(np.asarray(sl, np.float32)[None, :], (128, 8)))})
    resB = run_bass_kernel_spmd(ncB, in_maps, core_ids=list(range(NCORES)))
    attnT = np.empty((D, NTOK), np.float32)
    for k in range(NCORES):
        ohk = np.asarray(resB.results[k]["oh"])
        for j in range(8):
            idx = 8 * k + j
            bb, hh_ = idx // 16, idx % 16
            attnT[hh_ * 64:(hh_ + 1) * 64, bb * SEQ:(bb + 1) * SEQ] = ohk[j]

    ncC = build_C(TOK)
    in_maps = [{"xT": cols(xT, k), "atT": cols(attnT, k), "wgT": wgT, "wuT": wuT,
                "wdT": wdT, "wq": wq, "wk": wk, "wv": wv, "proj": proj,
                "g1v": g1v, "b1v": b1v_l1} for k in range(NCORES)]
    resC = run_bass_kernel_spmd(ncC, in_maps, core_ids=list(range(NCORES)))
    x2T = np.concatenate([np.asarray(r["x2T"]) for r in resC.results], axis=1)
    xinT = np.concatenate([np.asarray(r["xinT"]) for r in resC.results], axis=1)
    qpT = np.concatenate([np.asarray(r["qpT"]) for r in resC.results], axis=1)
    kpT = np.concatenate([np.asarray(r["kpT"]) for r in resC.results], axis=1)
    vpT = np.concatenate([np.asarray(r["vpT"]) for r in resC.results], axis=1)

    ncD = build_D(8)
    in_maps = []
    for k in range(NCORES):
        qs, ks, vs = [], [], []
        for j in range(8):
            idx = 8 * k + j
            bb, hh_ = idx // 16, idx % 16
            qs.append(qpT[hh_ * 64:(hh_ + 1) * 64, bb * SEQ:(bb + 1) * SEQ])
            ks.append(kpT[hh_ * 64:(hh_ + 1) * 64, bb * SEQ:(bb + 1) * SEQ])
            vs.append(vpT[hh_ * 64:(hh_ + 1) * 64, bb * SEQ:(bb + 1) * SEQ])
        in_maps.append({"qh": np.ascontiguousarray(np.stack(qs)),
                        "kh": np.ascontiguousarray(np.stack(ks)),
                        "vh": np.ascontiguousarray(np.stack(vs))})
    resD = run_bass_kernel_spmd(ncD, in_maps, core_ids=list(range(NCORES)))
    alT = np.empty((D, NTOK), bf)
    for k in range(NCORES):
        ohk = np.asarray(resD.results[k]["oh"])
        for j in range(8):
            idx = 8 * k + j
            bb, hh_ = idx // 16, idx % 16
            alT[hh_ * 64:(hh_ + 1) * 64, bb * SEQ:(bb + 1) * SEQ] = ohk[j]

    ncE = build_E(TOK)
    in_maps = [{"x2T": cols(x2T, k), "xinT": cols(xinT, k), "alT": cols(alT, k),
                "wo": wo, "w1": w1, "w2": w2, "wsT": wsT, "b1v": b1ffn,
                "b2v": b2ffn, "g2v": g2v, "be2v": be2v} for k in range(NCORES)]
    resE = run_bass_kernel_spmd(ncE, in_maps, core_ids=list(range(NCORES)))
    x4T = np.concatenate([np.asarray(r["x4T"]) for r in resE.results], axis=1)

    return np.ascontiguousarray(x4T.T).reshape(B, SEQ, D).astype(np.float32)
